# version 09
# baseline (speedup 1.0000x reference)
import numpy as np
import ml_dtypes
from contextlib import ExitStack

import concourse.bass as bass
import concourse.mybir as mybir
from concourse.bass_utils import run_bass_kernel_spmd

F32 = mybir.dt.float32
BF16 = mybir.dt.bfloat16
ALU = mybir.AluOpType
AF = mybir.ActivationFunctionType
AX = mybir.AxisListType
bfloat16 = ml_dtypes.bfloat16

D = 2048
ND = 16
TPC = 1024
NT = 8
FF = 8192
NF = 64
E = 8
CAP = 512.0
NCORE = 8
SCALE = float(128.0 ** -0.5)
BIG = 999.0

_RUN = run_bass_kernel_spmd


class SemC:

    def __init__(self, nc, CT, name):
        self.h = CT.enter_context(nc.semaphore(name))
        self.n = 0

    def inc(self, inst, k=1):
        inst.then_inc(self.h, k)
        self.n += k
        return inst


class Flush:

    def __init__(self, nc, CT, name, op=None):
        self.s = SemC(nc, CT, name)
        self.op = op

    def __call__(self, eng):
        inst = self.op(eng) if self.op is not None else eng.engine_nop()
        self.s.inc(inst)
        eng.wait_ge(self.s.h, self.s.n)


class SemPool:

    def __init__(self, nc, CT, name, nslots):
        self.ld = [SemC(nc, CT, f"{name}_ld{i}") for i in range(nslots)]
        self.fr = [SemC(nc, CT, f"{name}_fr{i}") for i in range(nslots)]
        self.nmax = nslots


class Stream2:

    def __init__(self, nc, CT, PS, name, shape, dtype, nslots=2, side="right", pool=None):
        self.bufs = [PS.enter_context(
            nc.sbuf_tensor(f"{name}{i}", list(shape), dtype, side=side))
            for i in range(nslots)]
        if pool is None:
            pool = SemPool(nc, CT, name, nslots)
        assert nslots <= pool.nmax
        self.ld = pool.ld[:nslots]
        self.fr = pool.fr[:nslots]
        self.ldb = [x.n for x in self.ld]
        self.frb = [x.n for x in self.fr]
        self.n = nslots
        self.ni = 0
        self.na = 0
        self.nc = 0

    def load(self, q, in_ap, transpose=False):
        i = self.ni
        s = i % self.n
        if i >= self.n:
            q.wait_ge(self.fr[s].h, self.frb[s] + i // self.n)
        elif self.frb[s] > 0:
            q.wait_ge(self.fr[s].h, self.frb[s])
            q.wait_ge(self.ld[s].h, self.ldb[s])
        kw = {"transpose": True} if transpose else {}
        q.dma_start(out=self.bufs[s][:], in_=in_ap, **kw).then_inc(self.ld[s].h, 16)
        self.ld[s].n += 16
        self.ni += 1

    def acquire(self, eng):
        i = self.na
        s = i % self.n
        eng.wait_ge(self.ld[s].h, self.ldb[s] + 16 * (i // self.n + 1))
        self.na += 1
        return self.bufs[s]

    def release(self, inst):
        s = self.nc % self.n
        inst.then_inc(self.fr[s].h, 1)
        self.fr[s].n += 1
        self.nc += 1

    def release_skip(self):
        self.fr[self.nc % self.n].n += 1
        self.nc += 1


def build_nc():
    nc = bass.Bass("TRN2", target_bir_lowering=False)
    dt_in = lambda n, s: nc.dram_tensor(n, list(s), F32, kind="ExternalInput").ap()
    dt_in16 = lambda n, s: nc.dram_tensor(n, list(s), BF16, kind="ExternalInput").ap()

    x_d = dt_in("x", [TPC, D])
    wq_d = dt_in("wq", [ND, 128, ND, 128])
    wk_d = dt_in("wk", [4, ND, 128, 512])
    wv_d = dt_in("wv", [4, ND, 128, 512])
    wo_d = dt_in("wo", [4, ND, 128, 512])
    w1_d = dt_in("w1", [NF, 128, ND, 128])
    w2_d = dt_in("w2", [NF, 128, D])
    rt_d = dt_in("rt_i", [128, ND * E])
    bq_d = dt_in("bq_i", [128, ND])
    b1_d = dt_in("b1_i", [128, NF])
    brows_d = dt_in("brows", [4, D])
    stairs_d = dt_in16("stairs_i", [128, 4 * 512])
    ones512_d = dt_in16("ones512_i", [128, 512])
    ones4f_d = dt_in("ones4f_i", [4, 128])
    id8_d = dt_in("id8_i", [8, 8])
    iota_d = dt_in("iota_i", [128, 128])
    selc_d = dt_in("selc_i", [8, 1])
    presel_d = dt_in("presel_i", [8, 128])
    aones8_d = dt_in("aones8_i", [8, 128])
    slotrow_d = dt_in("slotrow_i", [128, 512])
    stri64_d = dt_in("stri64_i", [64, 64])
    id64_d = dt_in("id64_i", [64, 64])
    osel_d = dt_in("osel_i", [64, 1])
    kron8_d = dt_in("kron8_i", [8, 64])

    out_d = nc.dram_tensor("out", [TPC, D], F32, kind="ExternalOutput").ap()
    xnh_d = nc.dram_tensor("xnh_s", [TPC, D], BF16).ap()
    xnl_d = nc.dram_tensor("xnl_s", [TPC, D], BF16).ap()
    s_d = nc.dram_tensor("s_scr", [TPC, D], BF16).ap()
    CCW = 2 * E + D + 4 * TPC
    cc_in = nc.dram_tensor("cc_in", [1, CCW], F32)
    cc_out = nc.dram_tensor("cc_out", [NCORE, CCW], F32)
    ccx_out = nc.dram_tensor("ccx_out", [NCORE * TPC, D], BF16, addr_space="Shared")
    cc3_in = [nc.dram_tensor(f"cc3_in{h}", [512, D // 2], BF16) for h in range(2)]
    cc3_out = [nc.dram_tensor(f"cc3_out{h}", [4096, D // 2], BF16, addr_space="Shared")
               for h in range(2)]
    scr_t = nc.dram_tensor("scr_t", [1, 512], F32).ap()
    scr_r = nc.dram_tensor("scr_r", [128, 8], F32).ap()

    with ExitStack() as CT:
        EC = CT.enter_context
        stairs = EC(nc.sbuf_tensor("stairs", [128, 2048], BF16))
        ones512 = EC(nc.sbuf_tensor("ones512", [128, 512], BF16))
        ones4f = EC(nc.sbuf_tensor("ones4f", [4, 128], F32))
        id8 = EC(nc.sbuf_tensor("id8", [8, 8], F32))
        iota2 = EC(nc.sbuf_tensor("iota2", [128, 128], F32))
        selc = EC(nc.sbuf_tensor("selc", [8, 1], F32))
        presel = EC(nc.sbuf_tensor("presel", [8, 128], F32))
        aones8 = EC(nc.sbuf_tensor("aones8", [8, 128], F32))
        stri64 = EC(nc.sbuf_tensor("stri64", [64, 64], F32))
        id64 = EC(nc.sbuf_tensor("id64", [64, 64], F32))
        osel = EC(nc.sbuf_tensor("osel", [64, 1], F32))
        kron8 = EC(nc.sbuf_tensor("kron8", [8, 64], F32))
        otab = EC(nc.sbuf_tensor("otab", [8, 16], F32))
        orow = EC(nc.sbuf_tensor("orow", [1, 64], F32))
        eq64 = EC(nc.sbuf_tensor("eq64", [128, 64], F32))
        idxa = EC(nc.sbuf_tensor("idxa", [128, 8], F32))
        idxb = EC(nc.sbuf_tensor("idxb", [128, 8], F32))
        kb64 = EC(nc.sbuf_tensor("kb64", [128, 64], BF16))
        P64 = EC(nc.sbuf_tensor("P64", [128, 64], F32))
        g64a = EC(nc.sbuf_tensor("g64a", [128, 64], F32))
        g64b = EC(nc.sbuf_tensor("g64b", [128, 64], F32))
        g64c = EC(nc.sbuf_tensor("g64c", [128, 64], F32))
        t64d = EC(nc.sbuf_tensor("t64d", [128, 64], F32))
        wrk64 = EC(nc.sbuf_tensor("wrk64", [128, 64], F32))
        exr = EC(nc.sbuf_tensor("exr", [64, 2], F32))
        oor = EC(nc.sbuf_tensor("oor", [1, 2], F32))
        oocol = EC(nc.sbuf_tensor("oocol", [128, 1], F32))
        ro8 = EC(nc.sbuf_tensor("ro8", [128, 8], F32))
        idq = EC(nc.sbuf_tensor("idq", [128, 32], mybir.dt.int16))
        ido = EC(nc.sbuf_tensor("ido", [128, 64], mybir.dt.int16))
        idqf = EC(nc.sbuf_tensor("idqf", [32, 32], F32))
        idof = EC(nc.sbuf_tensor("idof", [32, 64], F32))
        tokrow = EC(nc.sbuf_tensor("tokrow", [1, 512], F32))
        rt = EC(nc.sbuf_tensor("rt", [128, 128], F32))
        rthi = EC(nc.sbuf_tensor("rthi", [128, 128], BF16))
        rtlo = EC(nc.sbuf_tensor("rtlo", [128, 128], BF16))
        bqc = EC(nc.sbuf_tensor("bqc", [128, 16], F32))
        b1c = EC(nc.sbuf_tensor("b1c", [128, 64], F32))
        z_sb = EC(nc.sbuf_tensor("z_sb", [128, 64], F32))
        oh12 = EC(nc.sbuf_tensor("oh12", [128, 128], BF16))
        small = EC(nc.sbuf_tensor("small", [128, 372], F32))
        junk8 = small[:, 0:8]
        p1c = small[:, 8:16]
        p2c = small[:, 16:24]
        c1s = small[:, 24:32]
        c2s = small[:, 32:40]
        w_sb = small[:, 40:48]
        off1 = small[:, 48:56]
        off2 = small[:, 56:64]
        ssel = small[:, 64:80]
        tcol = [small[:, 80 + i:81 + i] for i in range(4)]
        epsc = small[:, 84:85]
        flcell = small[:, 85:86]
        junkB = small[:, 86:94]
        flcellB = small[:, 94:95]
        off1_64 = small[:, 96:160]
        off2_64 = small[:, 160:224]
        w64a = small[:, 224:288]
        w64b = small[:, 288:352]
        a8 = small[:, 352:360]
        b8 = small[:, 360:368]
        flc4 = [small[:, 368 + i:369 + i] for i in range(4)]
        n_sb = EC(nc.sbuf_tensor("n_sb", [1, 16], F32))
        gn_sb = EC(nc.sbuf_tensor("gn_sb", [8, 16], F32))
        bigA = EC(nc.sbuf_tensor("bigA", [128, 32768], BF16))
        bigB = EC(nc.sbuf_tensor("bigB", [128, 16384], BF16))
        xnT = EC(nc.sbuf_tensor("xnT", [128, 16384], BF16))
        PHIKV, PHIQ = 0, 16384

        pb = [EC(nc.psum_tensor(f"pb{i}", [128, 512], F32)) for i in range(8)]

        S = lambda name: SemC(nc, CT, name)
        s_k = S("s_k")
        s_lnv = S("s_lnv")
        s_lncp = S("s_lncp")
        s_lnsq = S("s_lnsq")
        s_lnb = S("s_lnb")
        s_lnr = S("s_lnr")
        s_xw = S("s_xw")
        s_bkv = S("s_bkv")
        s_bo = S("s_bo")
        s_b2 = S("s_b2")
        s_xt = S("s_xt")
        s_rt = S("s_rt")
        s_kmm = S("s_kmm")
        s_kev = S("s_kev")
        s_stt = S("s_stt")
        s_vmm = S("s_vmm")
        s_vev = S("s_vev")
        s_zmm = S("s_zmm")
        s_zev = S("s_zev")
        s_ztr = S("s_ztr")
        s_zte = S("s_zte")
        s_smm = S("s_smm")
        s_sev = S("s_sev")
        s_oh = S("s_oh")
        s_cmm = S("s_cmm")
        s_cev = S("s_cev")
        s_nmm = S("s_nmm")
        s_nev = S("s_nev")
        s_pub = S("s_pub")
        s_cc = S("s_cc")
        s_g = S("s_g")
        s_qmm = S("s_qmm")
        s_qev = S("s_qev")
        s_qstt = S("s_qstt")
        s_x7a = S("s_x7a")
        s_x7b = S("s_x7b")
        s_x7c = S("s_x7c")
        s_x7d = S("s_x7d")
        s_wch = S("s_wch")
        s_umm = S("s_umm")
        s_uev = S("s_uev")
        s_omm = S("s_omm")
        s_oev = S("s_oev")
        s_std = S("s_std0")
        s_std1 = S("s_std1")
        s_amm = S("s_amm")
        s_aev = S("s_aev")
        s_bmm = S("s_bmm")
        s_bwf = S("s_bwf")
        s_bev = S("s_bev")
        s_fin = S("s_fin")
        s_out = S("s_out")
        s_out1 = S("s_out1")
        s_ccx = S("s_ccx")
        s_tb = S("s_tb")
        s_kb = S("s_kb")
        s_ob = S("s_ob")
        s_obe = S("s_obe")
        s_pgm = S("s_pgm")
        s_pge = S("s_pge")
        s_exc = S("s_exc")
        s_cmpv = S("s_cmpv")
        s_cmpm = S("s_cmpm")
        s_tokm = S("s_tokm")
        s_toke = S("s_toke")
        s_owv = S("s_owv")
        s_bnc = S("s_bnc")
        s_idx = S("s_idx")
        s_xg = S("s_xg")
        s_fpv = S("s_fpv")
        s_fpd = S("s_fpd")
        s_fpd1 = S("s_fpd1")
        s_cc3 = S("s_cc3")
        s_cc3b = S("s_cc3b")
        s_fg = S("s_fg")
        s_fgb = S("s_fgb")
        s_gw = S("s_gw")
        poolG = SemPool(nc, CT, "pG", 4)
        poolS = SemPool(nc, CT, "pS", 2)
        poolT = SemPool(nc, CT, "pT", 2)
        poolU = SemPool(nc, CT, "pU", 2)
        vfl = Flush(nc, CT, "vfl")
        sfl = Flush(nc, CT, "sfl", op=lambda eng: eng.copy(flcell, epsc))

        block = EC(nc.Block())

        ones128 = ones512[:, 0:128]
        triu01 = stairs[:, 0:128]
        onescol = ones512[:, 0:1]

        def xslice(m, tb):
            return xnT[:, m * 1024 + tb * 128: m * 1024 + (tb + 1) * 128]

        def xhalf(m, th):
            return xnT[:, m * 1024 + th * 512: m * 1024 + (th + 1) * 512]

        def G8v(t):
            return t.rearrange("p (g e) -> p g e", e=8)

        def C8v(t):
            return t.rearrange("p (g o) -> p g o", o=1)

        def B8v(t):
            return C8v(t).broadcast_to([128, 8, 8])

        @block.sync
        def _(q):
            for dst, src in ((stairs, stairs_d), (ones512, ones512_d),
                             (ones4f, ones4f_d), (id8, id8_d), (iota2, iota_d),
                             (selc, selc_d), (presel, presel_d), (aones8, aones8_d),
                             (rt, rt_d), (bqc, bq_d), (b1c, b1_d),
                             (stri64, stri64_d), (kron8, kron8_d),
                             (id64, id64_d), (osel, osel_d)):
                s_k.inc(q.dma_start(out=dst[:], in_=src), 16)

        @block.vector
        def _(v):
            v.wait_ge(s_k.h, s_k.n)
            s_rt.inc(v.memset(epsc, 1e-5))
            s_rt.inc(v.tensor_copy(rthi[:], rt[:]))
            vfl(v)
            s_rt.inc(v.tensor_tensor(rtlo[:], rt[:], rthi[:], ALU.subtract))

        with ExitStack() as PO:
            gs_sb = PO.enter_context(nc.sbuf_tensor("gsS", [8, D], F32, side="right"))
            srow = PO.enter_context(nc.sbuf_tensor("srow", [1, D], F32, side="right"))

            with ExitStack() as PS:
                xc = PS.enter_context(nc.sbuf_tensor("xc", [128, D], F32, side="right"))
                xh1 = PS.enter_context(nc.sbuf_tensor("xh1", [128, D], BF16, side="right"))
                xl1 = PS.enter_context(nc.sbuf_tensor("xl1", [128, D], BF16, side="right"))
                lncol = PS.enter_context(nc.sbuf_tensor("lncol", [128, 8], F32, side="right"))
                xS = Stream2(nc, CT, PS, "xS", [128, D], F32, nslots=2, pool=poolS)
                sums, mu, ssq, t1, t2, varr, srt, rinv = (lncol[:, i:i + 1] for i in range(8))

                @block.sync
                def _(q):
                    for j in range(NT):
                        xS.load(q, x_d[j * 128:(j + 1) * 128, :])
                        if j >= 1:
                            q.wait_ge(s_lnv.h, j)
                            s_xw.inc(q.dma_start(out=xnh_d[(j - 1) * 128:j * 128, :], in_=xh1[:]), 16)
                            s_xw.inc(q.dma_start(out=xnl_d[(j - 1) * 128:j * 128, :], in_=xl1[:]), 16)
                    q.wait_ge(s_lnv.h, NT)
                    s_xw.inc(q.dma_start(out=xnh_d[(NT - 1) * 128:NT * 128, :], in_=xh1[:]), 16)
                    s_xw.inc(q.dma_start(out=xnl_d[(NT - 1) * 128:NT * 128, :], in_=xl1[:]), 16)
                    q.wait_ge(s_xw.h, s_xw.n)
                    for m in range(ND):
                        s_xt.inc(q.dma_start(
                            out=xnT[:, m * 1024:(m + 1) * 1024],
                            in_=xnh_d[0:TPC, m * 128:(m + 1) * 128],
                            transpose=True), 16)

                @block.scalar
                def _(sc):
                    sc.wait_ge(s_rt.h, 1)
                    for j in range(NT):
                        sc.wait_ge(xS.ld[j % 2].h, xS.ldb[j % 2] + 16 * (j // 2 + 1))
                        if j >= 1:
                            sc.wait_ge(s_lnv.h, j)
                        buf = xS.bufs[j % 2]
                        s_lncp.inc(sc.activation(xc[:], buf[:], AF.Copy, accum_out=sums))
                        sfl(sc)
                        s_lnsq.inc(sc.activation(xc[:], buf[:], AF.Square, accum_out=ssq))
                        sc.wait_ge(s_lnr.h, j + 1)
                        s_lnb.inc(sc.activation(srt, varr, AF.Sqrt, bias=epsc, scale=1.0))

                @block.vector
                def _(v):
                    inv_d = 1.0 / D
                    for j in range(NT):
                        v.wait_ge(xS.ld[j % 2].h, xS.ldb[j % 2] + 16 * (j // 2 + 1))
                        buf = xS.bufs[j % 2]
                        v.wait_ge(s_lncp.h, j + 1)
                        v.tensor_scalar_mul(mu, sums, inv_d)
                        v.wait_ge(s_lnsq.h, j + 1)
                        v.tensor_scalar_mul(t1, ssq, inv_d)
                        vfl(v)
                        v.tensor_mul(t2, mu, mu)
                        vfl(v)
                        s_lnr.inc(v.tensor_sub(varr, t1, t2))
                        v.wait_ge(s_lnb.h, j + 1)
                        v.reciprocal(rinv, srt)
                        if j >= 1:
                            v.wait_ge(s_xw.h, 32 * j)
                        vfl(v)
                        v.tensor_scalar(xh1[:], buf[:], mu, rinv, ALU.subtract, ALU.mult)
                        ti = v.tensor_scalar(xc[:], buf[:], mu, rinv, ALU.subtract, ALU.mult)
                        xS.release(ti)
                        vfl(v)
                        s_lnv.inc(v.tensor_tensor(xl1[:], xc[:], xh1[:], ALU.subtract))

            with ExitStack() as PS:
                phik = PS.enter_context(nc.sbuf_tensor("phik", [128, 8 * 512], BF16, side="right"))
                ekt = PS.enter_context(nc.sbuf_tensor("ekt", [128, 1024], F32, side="right"))
                rkt = PS.enter_context(nc.sbuf_tensor("rkt", [128, 1024], F32, side="right"))
                bk_r = PS.enter_context(nc.sbuf_tensor("bk_r", [1, D], BF16, side="right"))
                bv_r = PS.enter_context(nc.sbuf_tensor("bv_r", [1, D], BF16, side="right"))
                wkvS = Stream2(nc, CT, PS, "wkvS", [128, 512], BF16, nslots=4, pool=poolG)
                eslot = lambda t, u: t[:, (u % 2) * 512:((u % 2) + 1) * 512]

                @block.gpsimd
                def _(g):
                    g.wait_ge(s_lnv.h, 8)
                    g.wait_ge(s_lnsq.h, 8)
                    g.wait_ge(s_xw.h, 256)
                    s_bkv.inc(g.dma_start(out=bk_r[:], in_=brows_d[0:1, :]), 16)
                    s_bkv.inc(g.dma_start(out=bv_r[:], in_=brows_d[1:2, :]), 16)
                    for eg in range(4):
                        for m in range(ND):
                            wkvS.load(g, wk_d[eg, m])
                        for m in range(ND):
                            wkvS.load(g, wv_d[eg, m])
                    s_ccx.inc(g.collective_compute(
                        "AllGather", ALU.bypass,
                        replica_groups=[list(range(NCORE))],
                        ins=[xnh_d.opt()],
                        outs=[ccx_out.ap().opt()]))

                @block.tensor
                def _(t):
                    t.wait_ge(s_xt.h, 256)
                    t.wait_ge(s_k.h, s_k.n)
                    t.wait_ge(s_bkv.h, 32)
                    for eg in range(4):
                        if eg > 0:
                            t.wait_ge(s_vev.h, eg * 8)
                        for m in range(ND):
                            buf = wkvS.acquire(t)
                            for tb in range(NT):
                                ti = t.matmul(pb[tb][:], xslice(m, tb), buf[:],
                                              start=(m == 0), stop=False)
                            wkvS.release(ti)
                        for tb in range(NT):
                            ti = t.matmul(pb[tb][:], ones512[0:1, 0:128],
                                          bk_r[0:1, eg * 512:(eg + 1) * 512],
                                          start=False, stop=True)
                            s_kmm.inc(ti)
                        for m in range(ND):
                            buf = wkvS.acquire(t)
                            for tb in range(NT):
                                if m == 0:
                                    t.wait_ge(s_kev.h, eg * 8 + tb + 1)
                                ti = t.matmul(pb[tb][:], xslice(m, tb), buf[:],
                                              start=(m == 0), stop=False)
                            wkvS.release(ti)
                        for tb in range(NT):
                            ti = t.matmul(pb[tb][:], ones512[0:1, 0:128],
                                          bv_r[0:1, eg * 512:(eg + 1) * 512],
                                          start=False, stop=True)
                            s_vmm.inc(ti)

                @block.scalar
                def _(sc):
                    sc.wait_ge(s_rt.h, 1)
                    sfl(sc)
                    for eg in range(4):
                        for tb in range(NT):
                            u = eg * 8 + tb
                            sc.wait_ge(s_kmm.h, u + 1)
                            if u >= 2:
                                sc.wait_ge(s_stt.h, u - 1)
                            sc.activation(eslot(ekt, u), pb[tb][:], AF.Exp)
                            s_kev.inc(sc.activation(eslot(rkt, u), pb[tb][:], AF.Relu))

                @block.vector
                def _(v):
                    for eg in range(4):
                        vfl(v)
                        for tb in range(NT):
                            u = eg * 8 + tb
                            v.wait_ge(s_kev.h, u + 1)
                            s_stt.inc(v.scalar_tensor_tensor(
                                phik[:, tb * 512:(tb + 1) * 512],
                                eslot(ekt, u), 1.0, eslot(rkt, u), ALU.min, ALU.add))
                        vfl(v)
                        for tb in range(NT):
                            u = eg * 8 + tb
                            v.wait_ge(s_vmm.h, u + 1)
                            ti = v.tensor_tensor(
                                bigA[:, PHIKV + tb * 2048 + eg * 512: PHIKV + tb * 2048 + (eg + 1) * 512],
                                phik[:, tb * 512:(tb + 1) * 512], pb[tb][:], ALU.mult)
                            s_vev.inc(ti)

            with ExitStack() as PS:
                zfm = PS.enter_context(nc.sbuf_tensor("zfm", [8, 1024], F32, side="right"))
                xlS = Stream2(nc, CT, PS, "xlS", [128, 1024], BF16, nslots=2, pool=poolS)

                @block.sync
                def _(q):
                    q.wait_ge(s_vev.h, 32)
                    q.wait_ge(s_stt.h, 32)
                    for m in range(ND):
                        xlS.load(q, xnl_d[0:TPC, m * 128:(m + 1) * 128], transpose=True)

                @block.tensor
                def _(t):
                    t.wait_ge(s_vev.h, 32)
                    t.wait_ge(s_rt.h, 3)
                    ti = None
                    for m in range(ND):
                        buf = xlS.acquire(t)
                        for th in range(2):
                            st = (m == 0)
                            sp = (m == ND - 1)
                            zp = pb[th][0:8, :]
                            t.matmul(zp, rthi[:, m * 8:(m + 1) * 8], xhalf(m, th), start=st, stop=False)
                            t.matmul(zp, rtlo[:, m * 8:(m + 1) * 8], xhalf(m, th), start=False, stop=False)
                            ti = t.matmul(zp, rthi[:, m * 8:(m + 1) * 8],
                                          buf[:, th * 512:(th + 1) * 512], start=False, stop=sp)
                            if sp:
                                s_zmm.inc(ti)
                        if m < ND - 1:
                            xlS.release(ti)
                    for tb in range(NT):
                        t.wait_ge(s_zev.h, 2)
                        if tb >= 1:
                            t.wait_ge(s_zte.h, tb)
                        s_ztr.inc(t.transpose(pb[2][0:128, 0:8],
                                              zfm[0:8, tb * 128:(tb + 1) * 128], id8[:]))

                @block.vector
                def _(v):
                    for th in range(2):
                        v.wait_ge(s_zmm.h, th + 1)
                        s_zev.inc(v.tensor_copy(zfm[0:8, th * 512:(th + 1) * 512], pb[th][0:8, :]))
                    xlS.release(v.engine_nop())
                    for tb in range(NT):
                        v.wait_ge(s_ztr.h, tb + 1)
                        s_zte.inc(v.tensor_copy(z_sb[:, tb * 8:(tb + 1) * 8], pb[2][0:128, 0:8]))

            @block.tensor
            def _(t):
                for g2 in range(4):
                    bank = pb[4 + g2 % 2]
                    if g2 >= 2:
                        t.wait_ge(s_sev.h, g2 - 1)
                    for tb in range(NT):
                        ti = t.matmul(bank[0:1, :], onescol,
                                      bigA[:, PHIKV + tb * 2048 + g2 * 512: PHIKV + tb * 2048 + (g2 + 1) * 512],
                                      start=(tb == 0), stop=(tb == NT - 1))
                    s_smm.inc(ti)

            @block.vector
            def _(v):
                for g2 in range(4):
                    v.wait_ge(s_smm.h, g2 + 1)
                    s_sev.inc(v.tensor_copy(srow[0:1, g2 * 512:(g2 + 1) * 512],
                                            pb[4 + g2 % 2][0:1, :]))

            with ExitStack() as PS:
                r64a = PS.enter_context(nc.sbuf_tensor("r64a", [128, 64], F32, side="right"))
                r64b = PS.enter_context(nc.sbuf_tensor("r64b", [128, 64], F32, side="right"))
                r64c = PS.enter_context(nc.sbuf_tensor("r64c", [128, 64], F32, side="right"))
                rm8a = PS.enter_context(nc.sbuf_tensor("rm8a", [128, 8], F32, side="right"))
                rm8b = PS.enter_context(nc.sbuf_tensor("rm8b", [128, 8], F32, side="right"))
                iota64 = iota2[:, 0:64]
                iotaC64 = iota2[:, 64:128]
                oh1_64 = oh12[:, 0:64]
                oh2_64 = oh12[:, 64:128]

                @block.vector
                def _(v):
                    v.wait_ge(s_zte.h, 8)
                    vfl(v)
                    v.reduce_max(C8v(rm8a[:]), G8v(z_sb[:]), axis=AX.X)
                    vfl(v)
                    s_oh.inc(v.tensor_tensor(G8v(r64a[:]), G8v(z_sb[:]), B8v(rm8a[:]), ALU.subtract))

                @block.scalar
                def _(sc):
                    sfl(sc)
                    sc.wait_ge(s_oh.h, 1)
                    s_oh.inc(sc.activation(r64c[:], r64a[:], AF.Exp))

                @block.vector
                def _(v):
                    v.wait_ge(s_oh.h, 2)
                    v.reduce_sum(C8v(rm8a[:]), G8v(r64c[:]), axis=AX.X)
                    vfl(v)
                    v.reciprocal(rm8b[:], rm8a[:])
                    vfl(v)
                    v.tensor_tensor(G8v(r64b[:]), G8v(r64c[:]), B8v(rm8b[:]), ALU.mult)
                    vfl(v)
                    v.reduce_max(C8v(rm8a[:]), G8v(r64b[:]), axis=AX.X)
                    vfl(v)
                    v.tensor_tensor(G8v(r64a[:]), G8v(r64b[:]), B8v(rm8a[:]), ALU.is_ge)
                    vfl(v)
                    v.tensor_tensor(r64a[:], r64a[:], iotaC64, ALU.mult)
                    vfl(v)
                    v.tensor_single_scalar(r64a[:], r64a[:], BIG, ALU.add)
                    vfl(v)
                    v.tensor_reduce(C8v(idxa[:]), G8v(r64a[:]), AX.X, ALU.min)
                    vfl(v)
                    v.tensor_tensor(G8v(oh1_64), G8v(iota64), B8v(idxa[:]), ALU.is_equal)
                    vfl(v)
                    v.tensor_tensor(r64a[:], r64b[:], oh1_64, ALU.mult)
                    vfl(v)
                    v.reduce_sum(C8v(p1c[:]), G8v(r64a[:]), axis=AX.X)
                    v.scalar_tensor_tensor(r64c[:], oh1_64, -10000.0, r64b[:], ALU.mult, ALU.add)
                    vfl(v)
                    v.reduce_max(C8v(rm8a[:]), G8v(r64c[:]), axis=AX.X)
                    vfl(v)
                    v.tensor_tensor(G8v(r64a[:]), G8v(r64c[:]), B8v(rm8a[:]), ALU.is_ge)
                    vfl(v)
                    v.tensor_tensor(r64a[:], r64a[:], iotaC64, ALU.mult)
                    vfl(v)
                    v.tensor_single_scalar(r64a[:], r64a[:], BIG, ALU.add)
                    vfl(v)
                    v.tensor_reduce(C8v(idxb[:]), G8v(r64a[:]), AX.X, ALU.min)
                    vfl(v)
                    v.tensor_tensor(G8v(oh2_64), G8v(iota64), B8v(idxb[:]), ALU.is_equal)
                    vfl(v)
                    v.tensor_tensor(r64a[:], r64b[:], oh2_64, ALU.mult)
                    vfl(v)
                    s_oh.inc(v.reduce_sum(C8v(p2c[:]), G8v(r64a[:]), axis=AX.X))

                @block.tensor
                def _(t):
                    t.wait_ge(s_zte.h, 8)
                    t.wait_ge(s_oh.h, 3)
                    t.wait_ge(s_sev.h, 4)
                    for tbp in range(NT):
                        b1k = pb[6 + tbp % 2]
                        b2k = pb[4 + tbp % 2]
                        if tbp >= 2:
                            t.wait_ge(s_cev.h, tbp - 1)
                        for j in range(tbp + 1):
                            lhs = ones128 if j < tbp else triu01
                            t.matmul(b1k[:, 0:8], lhs, oh1_64[:, j * 8:(j + 1) * 8],
                                     start=(j == 0), stop=(j == tbp))
                            ti = t.matmul(b2k[:, 0:8], lhs, oh2_64[:, j * 8:(j + 1) * 8],
                                          start=(j == 0), stop=(j == tbp))
                        s_cmm.inc(ti)
                    t.wait_ge(s_cev.h, 8)
                    for tb in range(NT):
                        t.matmul(pb[4][0:1, 0:8], onescol, oh1_64[:, tb * 8:(tb + 1) * 8],
                                 start=(tb == 0), stop=(tb == NT - 1))
                        ti = t.matmul(pb[5][0:1, 0:8], onescol, oh2_64[:, tb * 8:(tb + 1) * 8],
                                      start=(tb == 0), stop=(tb == NT - 1))
                    s_nmm.inc(ti)

                @block.vector
                def _(v):
                    for tbp in range(NT):
                        b1k = pb[6 + tbp % 2]
                        b2k = pb[4 + tbp % 2]
                        v.wait_ge(s_cmm.h, tbp + 1)
                        vfl(v)
                        v.tensor_tensor(junk8, b1k[:, 0:8], oh1_64[:, tbp * 8:(tbp + 1) * 8], ALU.mult)
                        v.tensor_tensor(junkB, b2k[:, 0:8], oh2_64[:, tbp * 8:(tbp + 1) * 8], ALU.mult)
                        vfl(v)
                        v.reduce_sum(c1s[:, tbp:tbp + 1], junk8, axis=AX.X)
                        ti = v.reduce_sum(c2s[:, tbp:tbp + 1], junkB, axis=AX.X)
                        s_cev.inc(ti)
                    v.wait_ge(s_nmm.h, 1)
                    vfl(v)
                    v.tensor_copy(n_sb[0:1, 0:8], pb[4][0:1, 0:8])
                    s_nev.inc(v.tensor_copy(n_sb[0:1, 8:16], pb[5][0:1, 0:8]))

            @block.sync
            def _(q):
                q.wait_ge(s_sev.h, 4)
                q.wait_ge(s_nev.h, 1)
                s_pub.inc(q.dma_start(out=cc_in.ap()[0:1, 0:16], in_=n_sb[:]), 16)
                s_pub.inc(q.dma_start(out=cc_in.ap()[0:1, 16:16 + D], in_=srow[:]), 16)
                q.wait_ge(s_oh.h, 3)
                q.wait_ge(s_cev.h, 8)
                base = 16 + D
                for k, tsrc in enumerate((idxa, idxb, c1s, c2s)):
                    s_pub.inc(q.dma_start(
                        out=cc_in.ap()[0:1, base + k * TPC: base + (k + 1) * TPC]
                           .rearrange("o (p a) -> (o p) a", a=8),
                        in_=tsrc[:] if k < 2 else tsrc), 16)

            @block.gpsimd
            def _(g):
                g.wait_ge(s_pub.h, 96)
                s_cc.inc(g.collective_compute(
                    "AllGather", ALU.bypass,
                    replica_groups=[list(range(NCORE))],
                    ins=[cc_in.ap().opt()],
                    outs=[cc_out.ap().opt()]))

            @block.sync
            def _(q):
                q.wait_ge(s_cc.h, 1)
                s_g.inc(q.dma_start(out=gn_sb[:], in_=cc_out.ap()[:, 0:16]), 16)
                s_g.inc(q.dma_start(out=gs_sb[:], in_=cc_out.ap()[:, 16:16 + D]), 16)
                base = 16 + D
                for k, tdst in enumerate((g64a, g64b, g64c, t64d)):
                    s_tb.inc(q.dma_start(
                        out=tdst[:].rearrange("p (c a) -> p c a", a=8),
                        in_=cc_out.ap()[:, base + k * TPC: base + (k + 1) * TPC]
                            .rearrange("c (p a) -> p c a", p=128),
                        allow_slow_non_contiguous=True), 16)

            with ExitStack() as PS:
                eqt = PS.enter_context(nc.sbuf_tensor("eqt", [128, 2048], F32, side="right"))
                rqt = PS.enter_context(nc.sbuf_tensor("rqt", [128, 2048], F32, side="right"))
                wqS = Stream2(nc, CT, PS, "wqS", [128, 2048], BF16, nslots=3, pool=poolG)
                eslot = lambda t, u: t[:, (u % 4) * 512:((u % 4) + 1) * 512]

                @block.gpsimd
                def _(g):
                    g.wait_ge(s_oh.h, 3)
                    for ec in range(ND):
                        wqS.load(g, wq_d[ec])

                @block.tensor
                def _(t):
                    t.wait_ge(s_zev.h, 2)
                    for ec in range(ND):
                        buf = wqS.acquire(t)
                        for th in range(2):
                            u = ec * 2 + th
                            if u >= 4:
                                t.wait_ge(s_qev.h, u - 3)
                            bank = pb[u % 4]
                            for m in range(ND):
                                ti = t.matmul(bank[:], buf[:, m * 128:(m + 1) * 128],
                                              xhalf(m, th), start=(m == 0), stop=(m == ND - 1))
                            s_qmm.inc(ti)

                @block.scalar
                def _(sc):
                    sfl(sc)
                    for ec in range(ND):
                        for th in range(2):
                            u = ec * 2 + th
                            sc.wait_ge(s_qmm.h, u + 1)
                            if u >= 4:
                                sc.wait_ge(s_qstt.h, u - 3)
                            bank = pb[u % 4]
                            ei = sc.activation(eslot(eqt, u), bank[:], AF.Exp,
                                               bias=bqc[:, ec:ec + 1], scale=1.0)
                            if th == 1:
                                wqS.release(ei)
                            s_qev.inc(sc.activation(eslot(rqt, u), bank[:], AF.Relu,
                                                    bias=bqc[:, ec:ec + 1], scale=1.0))

                @block.vector
                def _(v):
                    for ec in range(ND):
                        for th in range(2):
                            u = ec * 2 + th
                            v.wait_ge(s_qev.h, u + 1)
                            ti = v.scalar_tensor_tensor(
                                bigA[:, PHIQ + ec * 1024 + th * 512: PHIQ + ec * 1024 + (th + 1) * 512],
                                eslot(eqt, u), 1.0, eslot(rqt, u), ALU.min, ALU.add)
                            s_qstt.inc(ti)

            @block.tensor
            def _(t):
                t.wait_ge(s_g.h, 32)
                t.wait_ge(s_qev.h, 32)
                for m in range(ND):
                    ti = t.matmul(pb[2][:, m:m + 1], gs_sb[0:8, m * 128:(m + 1) * 128],
                                  selc[:], start=True, stop=True)
                s_x7a.inc(ti)

            @block.vector
            def _(v):
                v.wait_ge(s_x7a.h, 1)
                s_x7b.inc(v.tensor_copy(ssel, pb[2][:, 0:16]))

            @block.tensor
            def _(t):
                t.wait_ge(s_x7b.h, 1)
                for g in range(8):
                    ti = t.matmul(pb[2][:, 64 + g * 8: 72 + g * 8], presel[:], gn_sb[0:8, 0:8],
                                  start=True, stop=True)
                for g in range(8):
                    t.matmul(pb[3][:, g * 8:(g + 1) * 8], aones8[:], gn_sb[0:8, 0:8],
                             start=True, stop=False)
                    ti = t.matmul(pb[3][:, g * 8:(g + 1) * 8], presel[:], gn_sb[0:8, 8:16],
                                  start=False, stop=True)
                s_x7c.inc(ti)

            @block.vector
            def _(v):
                v.wait_ge(s_x7c.h, 1)
                v.tensor_copy(off1_64, pb[2][:, 64:128])
                s_x7d.inc(v.tensor_copy(off2_64, pb[3][:, 0:64]))
                vfl(v)
                v.tensor_tensor(w64a, off1_64, oh12[:, 0:64], ALU.mult)
                v.tensor_tensor(w64b, off2_64, oh12[:, 64:128], ALU.mult)
                vfl(v)
                v.reduce_sum(C8v(a8), G8v(w64a), axis=AX.X)
                v.reduce_sum(C8v(b8), G8v(w64b), axis=AX.X)
                vfl(v)
                v.tensor_tensor(a8, a8, c1s[:], ALU.add)
                v.tensor_tensor(b8, b8, c2s[:], ALU.add)
                vfl(v)
                v.tensor_single_scalar(a8, a8, CAP + 0.5, ALU.is_lt)
                v.tensor_single_scalar(b8, b8, CAP + 0.5, ALU.is_lt)
                vfl(v)
                v.tensor_tensor(a8, a8, p1c[:], ALU.mult)
                v.tensor_tensor(b8, b8, p2c[:], ALU.mult)
                vfl(v)
                s_wch.inc(v.tensor_tensor(w_sb[:], a8, b8, ALU.add))

        with ExitStack() as PS:
            ctmp = PS.enter_context(nc.sbuf_tensor("ctmp", [128, 2048], F32, side="right"))
            cslot = lambda u: ctmp[:, (u % 4) * 512:((u % 4) + 1) * 512]

            @block.tensor
            def _(t):
                t.wait_ge(s_qev.h, 32)
                t.wait_ge(s_x7d.h, 1)
                for m in range(ND):
                    for g2 in range(2):
                        u = m * 2 + g2
                        if u >= 4:
                            t.wait_ge(s_uev.h, u - 3)
                        bank = pb[u % 4]
                        jmax = 4 * g2 + 3
                        for j in range(jmax + 1):
                            rhs = (ones512[:] if j < 4 * g2
                                   else stairs[:, (j - 4 * g2) * 512:(j - 4 * g2 + 1) * 512])
                            ti = t.matmul(bank[:],
                                          bigA[:, PHIKV + j * 2048 + m * 128: PHIKV + j * 2048 + (m + 1) * 128],
                                          rhs, start=(j == 0), stop=(j == jmax))
                        s_umm.inc(ti)

            @block.vector
            def _(v):
                v.wait_ge(s_x7b.h, 1)
                v.wait_ge(s_qstt.h, 32)
                for m in range(ND):
                    for g2 in range(2):
                        u = m * 2 + g2
                        v.wait_ge(s_umm.h, u + 1)
                        bank = pb[u % 4]
                        v.scalar_tensor_tensor(
                            cslot(u), bank[:], ssel[:, m:m + 1],
                            bigA[:, PHIQ + m * 1024 + g2 * 512: PHIQ + m * 1024 + (g2 + 1) * 512],
                            ALU.add, ALU.mult)
                        vfl(v)
                        ti = v.tensor_scalar_mul(
                            bigB[:, m * 1024 + g2 * 512: m * 1024 + (g2 + 1) * 512],
                            cslot(u), SCALE)
                        s_uev.inc(ti)

        @block.tensor
        def _(t):
            t.wait_ge(s_g.h, 32)
            t.wait_ge(s_nev.h, 1)
            t.wait_ge(s_sev.h, 4)
            ti = t.matmul(pb[4][0:8, 0:8], stri64[0:8, 0:8], gn_sb[0:8, 0:8],
                          start=True, stop=True)
            t.matmul(pb[4][0:8, 8:16], aones8[0:8, 0:8], gn_sb[0:8, 0:8],
                     start=True, stop=False)
            ti = t.matmul(pb[4][0:8, 8:16], stri64[0:8, 0:8], gn_sb[0:8, 8:16],
                          start=False, stop=True)
            s_ob.inc(ti)
            for k in range(2):
                for e in range(E):
                    j = k * 8 + e
                    t.wait_ge(s_obe.h, j + 1)
                    ti = t.matmul(pb[5][0:1, 0:64], otab[0:8, k * 8 + e:k * 8 + e + 1],
                                  kron8[:], start=True, stop=True)
                    s_ob.inc(ti)
                    t.wait_ge(s_obe.h, j + 2)
                    ti = t.matmul(pb[4][0:128, 16:80], ones4f[0:1, :], orow[:],
                                  start=True, stop=True)
                    s_ob.inc(ti)

        @block.vector
        def _(v):
            v.wait_ge(s_tb.h, 64)
            v.wait_ge(s_ob.h, 1)
            vfl(v)
            v.tensor_copy(otab[0:8, 0:16], pb[4][0:8, 0:16])
            s_obe.inc(v.engine_nop())
            for k in range(2):
                src = g64a if k == 0 else g64b
                acc = g64c if k == 0 else t64d
                for e in range(E):
                    j = k * 8 + e
                    v.wait_ge(s_ob.h, 2 + 2 * j)
                    vfl(v)
                    v.tensor_copy(orow[:], pb[5][0:1, 0:64])
                    s_obe.inc(v.engine_nop())
                    v.wait_ge(s_ob.h, 3 + 2 * j)
                    v.tensor_single_scalar(eq64[:], src[:], float(e), ALU.is_equal)
                    vfl(v)
                    v.tensor_tensor(wrk64[:], eq64[:], pb[4][0:128, 16:80], ALU.mult)
                    vfl(v)
                    ti = v.tensor_tensor(acc[:], acc[:], wrk64[:], ALU.add)
                    vfl(v)
            v.tensor_tensor(g64c[:], g64c[:], t64d[:], ALU.min)
            vfl(v)
            s_kb.inc(v.tensor_single_scalar(kb64[:], g64c[:], CAP + 0.5, ALU.is_lt))
            v.wait_ge(s_wch.h, 1)
            vfl(v)
            s_kb.inc(v.tensor_single_scalar(oh12[:, 0:8], w_sb[:], 0.0, ALU.is_gt))

        with ExitStack() as PS:
            bo_r = PS.enter_context(nc.sbuf_tensor("bo_r", [1, D], BF16, side="right"))
            sstage = PS.enter_context(nc.sbuf_tensor("sstage", [128, 1024], BF16, side="right"))
            woS = Stream2(nc, CT, PS, "woS", [128, 512], BF16, nslots=4, pool=poolG)
            sslot = lambda u: sstage[:, (u % 2) * 512:((u % 2) + 1) * 512]



            @block.gpsimd
            def _(g):
                g.wait_ge(s_uev.h, 32)
                s_bo.inc(g.dma_start(out=bo_r[:], in_=brows_d[2:3, :]), 16)
                for eg in range(4):
                    for m in range(ND):
                        woS.load(g, wo_d[eg, m])

            @block.tensor
            def _(t):
                t.wait_ge(s_uev.h, 32)
                t.wait_ge(s_x7d.h, 1)
                t.wait_ge(s_nev.h, 1)
                t.wait_ge(s_cev.h, 8)
                t.wait_ge(s_kb.h, 1)
                t.wait_ge(s_bo.h, 16)
                for eg in range(4):
                    if eg > 0:
                        t.wait_ge(s_oev.h, eg * 8)
                    for m in range(ND):
                        buf = woS.acquire(t)
                        for tb in range(NT):
                            ti = t.matmul(pb[tb][:],
                                          bigB[:, m * 1024 + tb * 128: m * 1024 + (tb + 1) * 128],
                                          buf[:], start=(m == 0), stop=False)
                        woS.release(ti)
                    for tb in range(NT):
                        ti = t.matmul(pb[tb][:], ones512[0:1, 0:128],
                                      bo_r[0:1, eg * 512:(eg + 1) * 512],
                                      start=False, stop=True)
                        s_omm.inc(ti)

            @block.vector
            def _(v):
                for eg in range(4):
                    for tb in range(NT):
                        u = eg * 8 + tb
                        v.wait_ge(s_omm.h, u + 1)
                        if u >= 2:
                            v.wait_ge((s_std if u % 2 == 0 else s_std1).h, 16 * (u // 2))
                        ti = v.tensor_copy(sslot(u), pb[tb][:])
                        s_oev.inc(ti)

            @block.sync
            def _(q):
                for eg in range(4):
                    for tb in range(NT):
                        u = eg * 8 + tb
                        q.wait_ge(s_oev.h, u + 1)
                        (s_std if u % 2 == 0 else s_std1).inc(q.dma_start(
                            out=s_d[tb * 128:(tb + 1) * 128, eg * 512:(eg + 1) * 512],
                            in_=sslot(u)), 16)

        with ExitStack() as PS:
            cmp_sb = PS.enter_context(nc.sbuf_tensor("cmp_sb", [128, 2048], BF16, side="right"))
            slotrow = PS.enter_context(nc.sbuf_tensor("slotrow", [128, 512], F32, side="right"))

            @block.sync
            def _(q):
                q.wait_ge(s_std.h, 256)
                q.wait_ge(s_std1.h, 256)
                s_k.inc(q.dma_start(out=slotrow[:], in_=slotrow_d), 16)

            @block.tensor
            def _(t):
                t.wait_ge(s_oev.h, 32)
                t.wait_ge(s_kb.h, 2)
                kbo = oh12[:, 0:8]
                ti = t.matmul(pb[0][0:1, 0:64], onescol, kb64[:], start=True, stop=True)
                s_pgm.inc(ti)
                t.wait_ge(s_pge.h, 1)
                ti = t.transpose(pb[0][0:64, 64:65], wrk64[0:1, 0:64], ones4f[0:1, 0:1])
                s_pgm.inc(ti)
                t.wait_ge(s_pge.h, 2)
                ti = t.matmul(pb[0][0:64, 65:66], stri64[:], exr[0:64, 0:1], start=True, stop=True)
                s_pgm.inc(ti)
                t.wait_ge(s_pge.h, 3)
                ti = t.transpose(pb[0][0:1, 66:130], exr[0:64, 1:2], id64[:])
                s_pgm.inc(ti)
                ti = t.matmul(pb[0][0:1, 130:131], exr[0:64, 1:2], osel[:], start=True, stop=True)
                s_pgm.inc(ti)
                t.wait_ge(s_pge.h, 5)
                ti = t.matmul(pb[0][0:128, 131:132], ones4f[0:1, :], oor[0:1, 0:1], start=True, stop=True)
                s_pgm.inc(ti)
                t.matmul(pb[1][:, 0:64], triu01, kb64[:], start=True, stop=False)
                ti = t.matmul(pb[1][:, 0:64], ones4f[0:1, :], wrk64[0:1, 0:64], start=False, stop=True)
                s_pgm.inc(ti)
                ti = t.matmul(pb[2][0:1, 0:8], onescol, kbo, start=True, stop=True)
                s_pgm.inc(ti)
                t.wait_ge(s_pge.h, 8)
                ti = t.transpose(pb[2][0:8, 8:9], g64a[0:1, 0:8], ones4f[0:1, 0:1])
                s_pgm.inc(ti)
                t.wait_ge(s_pge.h, 9)
                ti = t.matmul(pb[2][0:8, 9:10], stri64[0:8, 0:8], g64b[0:8, 0:1], start=True, stop=True)
                s_pgm.inc(ti)
                t.wait_ge(s_pge.h, 10)
                ti = t.transpose(pb[2][0:1, 10:18], g64b[0:8, 1:2], id8[:])
                s_pgm.inc(ti)
                t.wait_ge(s_pge.h, 11)
                t.matmul(pb[2][:, 18:26], triu01, kbo, start=True, stop=False)
                ti = t.matmul(pb[2][:, 18:26], ones4f[0:1, :], g64a[0:1, 8:16], start=False, stop=True)
                s_pgm.inc(ti)
                t.wait_ge(s_pge.h, 13)
                for col in range(64):
                    t.wait_ge(s_cmpv.h, col + 1)
                    ti = t.matmul(pb[3][0:1, 0:512], onescol,
                                  cmp_sb[:, (col % 4) * 512:((col % 4) + 1) * 512],
                                  start=(col == 0), stop=(col == 63))
                    s_cmpm.inc(ti)

            @block.vector
            def _(v):
                v.wait_ge(s_pgm.h, 1)
                vfl(v)
                v.tensor_copy(wrk64[0:1, 0:64], pb[0][0:1, 0:64])
                s_pge.inc(v.engine_nop())
                v.wait_ge(s_pgm.h, 2)
                v.tensor_copy(exr[0:64, 0:1], pb[0][0:64, 64:65])
                s_pge.inc(v.engine_nop())
                v.wait_ge(s_pgm.h, 3)
                v.tensor_copy(exr[0:64, 1:2], pb[0][0:64, 65:66])
                s_pge.inc(v.engine_nop())
                v.wait_ge(s_pgm.h, 5)
                vfl(v)
                v.tensor_copy(wrk64[0:1, 0:64], pb[0][0:1, 66:130])
                s_pge.inc(v.engine_nop())
                v.tensor_copy(oor[0:1, 0:1], pb[0][0:1, 130:131])
                s_pge.inc(v.engine_nop())
                v.wait_ge(s_pgm.h, 6)
                v.tensor_copy(oocol[:], pb[0][0:128, 131:132])
                s_pge.inc(v.engine_nop())
                v.wait_ge(s_pgm.h, 8)
                vfl(v)
                v.tensor_copy(g64a[0:1, 0:8], pb[2][0:1, 0:8])
                s_pge.inc(v.engine_nop())
                s_pge.inc(v.engine_nop())
                v.wait_ge(s_pgm.h, 9)
                vfl(v)
                v.tensor_copy(g64b[0:8, 0:1], pb[2][0:8, 8:9])
                s_pge.inc(v.engine_nop())
                v.wait_ge(s_pgm.h, 10)
                vfl(v)
                v.tensor_copy(g64b[0:8, 1:2], pb[2][0:8, 9:10])
                s_pge.inc(v.engine_nop())
                v.wait_ge(s_pgm.h, 11)
                vfl(v)
                v.tensor_copy(g64a[0:1, 8:16], pb[2][0:1, 10:18])
                s_pge.inc(v.engine_nop())
                v.wait_ge(s_pgm.h, 7)
                vfl(v)
                v.tensor_copy(P64[:], pb[1][:, 0:64])
                s_pge.inc(v.engine_nop())
                s_pge.inc(v.engine_nop())
                v.wait_ge(s_pgm.h, 12)
                vfl(v)
                v.tensor_scalar(ro8[:], pb[2][:, 18:26], oocol[:], 1.0, ALU.add, ALU.subtract)
                vfl(v)
                v.tensor_scalar(ro8[:], ro8[:], 0.0, 4095.0, ALU.max, ALU.min)
                s_owv.inc(v.engine_nop())
                v.wait_ge(s_k.h, s_k.n)
                for col in range(64):
                    if col >= 4:
                        v.wait_ge(s_cmpm.h, col - 3)
                    vfl(v)
                    ti = v.tensor_tensor(
                        cmp_sb[:, (col % 4) * 512:((col % 4) + 1) * 512],
                        P64[:, col:col + 1].broadcast_to([128, 512]),
                        slotrow[:], ALU.is_le)
                    s_cmpv.inc(ti)
                v.wait_ge(s_cmpm.h, 64)
                vfl(v)
                ti = v.tensor_single_scalar(tokrow[:], pb[3][0:1, 0:512], 8191.0, ALU.min)
                s_toke.inc(ti)

            @block.sync
            def _(q):
                q.wait_ge(s_owv.h, 1)
                s_bnc.inc(q.dma_start(out=scr_r, in_=ro8[:]), 16)
                q.wait_ge(s_toke.h, 1)
                s_bnc.inc(q.dma_start(out=scr_t, in_=tokrow[:]), 16)
                q.wait_ge(s_bnc.h, 32)
                for h in range(2):
                    s_bnc.inc(q.dma_start(
                        out=idqf[16 * h:16 * (h + 1), :],
                        in_=scr_t[0:1, 0:512].rearrange("o (c p) -> (o p) c", p=16),
                        allow_slow_non_contiguous=True), 16)
                    s_bnc.inc(q.dma_start(
                        out=idof[16 * h:16 * (h + 1), :].rearrange("q (a b) -> q a b", b=8),
                        in_=scr_r.rearrange("(b q) a -> q a b", q=16),
                        allow_slow_non_contiguous=True), 16)

            @block.vector
            def _(v):
                v.wait_ge(s_bnc.h, 96)
                vfl(v)
                v.memset(idq[:], 0)
                v.memset(ido[:], 0)
                vfl(v)
                v.tensor_copy(idq[0:32, :], idqf[:])
                ti = v.tensor_copy(ido[0:32, :], idof[:])
                s_idx.inc(ti)

        with ExitStack() as PS:
            _credit = [(p.h, p.n) for p in poolG.ld + poolT.ld + poolS.ld + poolU.ld]
            x2S = Stream2(nc, CT, PS, "x2S", [128, D // 2], F32, nslots=2, pool=poolS)
            s2S = Stream2(nc, CT, PS, "s2S", [128, D // 2], BF16, nslots=2, pool=poolU)
            b2_r = PS.enter_context(nc.sbuf_tensor("b2_r", [1, D], BF16, side="right"))
            fpub = PS.enter_context(nc.sbuf_tensor("fpub", [128, 1024], BF16, side="right"))
            wft = PS.enter_context(nc.sbuf_tensor("wft", [128, 1024], F32, side="right"))
            w1S = Stream2(nc, CT, PS, "w1S", [128, 2048], BF16, nslots=2, pool=poolG)
            w2S = Stream2(nc, CT, PS, "w2S", [128, 1024], BF16, nslots=2, pool=poolT)
            sg = PS.enter_context(nc.sbuf_tensor("sg", [128, D // 2], F32, side="right"))
            ot = PS.enter_context(nc.sbuf_tensor("ot", [128, D], F32, side="right"))
            oslot = lambda u: ot[:, (u % 2) * 1024:((u % 2) + 1) * 1024]

            @block.sync
            def _(q):
                q.wait_ge(s_std.h, 256)
                q.wait_ge(s_std1.h, 256)
                q.wait_ge(s_omm.h, 32)
                for tb in range(2):
                    x2S.load(q, x_d[tb * 128:(tb + 1) * 128, 0:D // 2])
                    s2S.load(q, s_d[tb * 128:(tb + 1) * 128, 0:D // 2])
            fslot = lambda u: fpub[:, (u % 2) * 512:((u % 2) + 1) * 512]
            wslot = lambda u: wft[:, (u % 2) * 512:((u % 2) + 1) * 512]

            @block.gpsimd
            def _(g):
                g.wait_ge(s_oev.h, 32)
                g.wait_ge(s_std.h, 256)
                g.wait_ge(s_std1.h, 256)
                g.wait_ge(s_cmpm.h, 64)
                s_b2.inc(g.dma_start(out=b2_r[:], in_=brows_d[3:4, :]), 16)
                g.wait_ge(s_ccx.h, 1)
                g.wait_ge(s_idx.h, 1)
                g.wait_ge(s_omm.h, 32)
                s_xg.inc(g.dma_gather(
                    bigB[:, 0:8192].rearrange("p (f i) -> p f i", i=512),
                    ccx_out.ap()[:, :], idq[:], 512, 512, D, transpose=True), 16)
                for fc in range(NF):
                    w1S.load(g, w1_d[fc])
                for u in range(2 * NF):
                    eh, fc = divmod(u, NF)
                    if u == NF + 10:
                        g.wait_ge(s_fpd.h, 64)
                        g.wait_ge(s_fpd1.h, 64)
                        s_cc3.inc(g.collective_compute(
                            "AllGather", ALU.bypass,
                            replica_groups=[list(range(NCORE))],
                            ins=[cc3_in[0].ap().opt()],
                            outs=[cc3_out[0].ap().opt()]))
                    if u == NF + 34:
                        g.wait_ge(s_cc3.h, 1)
                        g.wait_ge(s_amm.h, NF)
                        s_fg.inc(g.dma_gather(
                            bigB[:, 8192:16384].rearrange("p (b e) -> p b e", e=D // 2),
                            cc3_out[0].ap()[:, :], ido[:], 1024, 1024, D // 2,
                            transpose=False), 16)
                    w2S.load(g, w2_d[fc, :, eh * 1024:(eh + 1) * 1024])
                g.wait_ge(s_fg.h, 16)
                g.wait_ge(s_wch.h, 1)
                for tb in range(NT):
                    for eg2 in range(2):
                        u = tb * 2 + eg2
                        if u >= 2:
                            g.wait_ge(s_bev.h, u - 1)
                        ti = g.tensor_scalar_mul(
                            wslot(u),
                            bigB[:, 8192 + tb * 1024 + eg2 * 512:
                                 8192 + tb * 1024 + (eg2 + 1) * 512],
                            w_sb[:, tb:tb + 1])
                        s_gw.inc(ti)
                g.wait_ge(s_fpd.h, 128)
                g.wait_ge(s_fpd1.h, 128)
                s_cc3b.inc(g.collective_compute(
                    "AllGather", ALU.bypass,
                    replica_groups=[list(range(NCORE))],
                    ins=[cc3_in[1].ap().opt()],
                    outs=[cc3_out[1].ap().opt()]))
                g.wait_ge(s_cc3b.h, 1)
                g.wait_ge(s_bmm.h, 16)
                s_fgb.inc(g.dma_gather(
                    bigA[:, 0:8192].rearrange("p (b e) -> p b e", e=D // 2),
                    cc3_out[1].ap()[:, :], ido[:], 1024, 1024, D // 2,
                    transpose=False), 16)
                g.wait_ge(s_fgb.h, 16)
                for tb in range(NT):
                    for eg2 in range(2):
                        u = 16 + tb * 2 + eg2
                        g.wait_ge(s_bev.h, u - 1)
                        ti = g.tensor_scalar_mul(
                            wslot(u),
                            bigA[:, tb * 1024 + eg2 * 512:
                                 tb * 1024 + (eg2 + 1) * 512],
                            w_sb[:, tb:tb + 1])
                        s_gw.inc(ti)

            @block.tensor
            def _(t):
                t.wait_ge(s_xg.h, 16)
                for fc in range(NF):
                    buf = w1S.acquire(t)
                    if fc >= 4:
                        t.wait_ge(s_aev.h, fc - 3)
                    bank = pb[fc % 4]
                    for m in range(ND):
                        ti = t.matmul(bank[:], buf[:, m * 128:(m + 1) * 128],
                                      bigB[:, m * 512:(m + 1) * 512],
                                      start=(m == 0), stop=(m == ND - 1))
                    s_amm.inc(ti)

            @block.scalar
            def _(sc):
                sc.wait_ge(s_uev.h, 32)
                for fc in range(NF):
                    sc.wait_ge(s_amm.h, fc + 1)
                    ti = sc.activation(bigA[:, fc * 512:(fc + 1) * 512],
                                       pb[fc % 4][:], AF.Gelu,
                                       bias=b1c[:, fc:fc + 1], scale=1.0)
                    w1S.release(ti)
                    s_aev.inc(sc.copy(flc4[fc % 4], epsc))

            @block.tensor
            def _(t):
                t.wait_ge(s_aev.h, NF)
                t.wait_ge(s_b2.h, 16)
                for eh in range(2):
                    if eh > 0:
                        t.wait_ge(s_fpv.h, 8)
                    for fc in range(NF):
                        buf = w2S.acquire(t)
                        for tb4 in range(4):
                            for eg2 in range(2):
                                ti = t.matmul(pb[tb4 * 2 + eg2][:],
                                              bigA[:, fc * 512 + tb4 * 128: fc * 512 + (tb4 + 1) * 128],
                                              buf[:, eg2 * 512:(eg2 + 1) * 512],
                                              start=(fc == 0), stop=False)
                        w2S.release(ti)
                    for tb4 in range(4):
                        for eg2 in range(2):
                            ti = t.matmul(pb[tb4 * 2 + eg2][:], ones512[0:1, 0:128],
                                          b2_r[0:1, eh * 1024 + eg2 * 512: eh * 1024 + (eg2 + 1) * 512],
                                          start=False, stop=True)
                            s_bmm.inc(ti)

            @block.vector
            def _(v):
                for eh in range(2):
                    for tb4 in range(4):
                        for eg2 in range(2):
                            u8 = tb4 * 2 + eg2
                            uu = eh * 8 + u8
                            v.wait_ge(s_bmm.h, uu + 1)
                            if uu >= 2:
                                v.wait_ge((s_fpd if uu % 2 == 0 else s_fpd1).h,
                                          16 * (uu // 2))
                            vfl(v)
                            ti = v.tensor_copy(fslot(uu), pb[u8][:])
                            s_fpv.inc(ti)

            @block.sync
            def _(q):
                for eh in range(2):
                    for tb4 in range(4):
                        for eg2 in range(2):
                            uu = eh * 8 + tb4 * 2 + eg2
                            q.wait_ge(s_fpv.h, uu + 1)
                            (s_fpd if uu % 2 == 0 else s_fpd1).inc(q.dma_start(
                                out=cc3_in[eh].ap()[tb4 * 128:(tb4 + 1) * 128,
                                                    eg2 * 512:(eg2 + 1) * 512],
                                in_=fslot(uu)), 16)

            @block.scalar
            def _(sc):
                sc.wait_ge(s_omm.h, 32)
                for h in range(2):
                    for tb in range(NT):
                        for eg2 in range(2):
                            u = h * 16 + tb * 2 + eg2
                            sc.wait_ge(s_gw.h, u + 1)
                            ti = sc.activation(
                                bigB[:, tb * 2048 + h * 1024 + eg2 * 512:
                                     tb * 2048 + h * 1024 + (eg2 + 1) * 512],
                                wslot(u), AF.Silu)
                            s_bev.inc(ti)

            def _uslc(u):
                h, tb = divmod(u, NT)
                return (tb * 128, (tb + 1) * 128, h * (D // 2), (h + 1) * (D // 2))

            @block.sync
            def _(q):
                for u in range(2, 16):
                    r0, r1, c0, c1 = _uslc(u)
                    x2S.load(q, x_d[r0:r1, c0:c1])
                    s2S.load(q, s_d[r0:r1, c0:c1])
                    q.wait_ge(s_fin.h, u - 1)
                    r0, r1, c0, c1 = _uslc(u - 2)
                    (s_out if u % 2 == 0 else s_out1).inc(
                        q.dma_start(out=out_d[r0:r1, c0:c1], in_=oslot(u - 2)), 16)
                for u in range(15, 17):
                    q.wait_ge(s_fin.h, u)
                    r0, r1, c0, c1 = _uslc(u - 1)
                    (s_out if u % 2 == 1 else s_out1).inc(
                        q.dma_start(out=out_d[r0:r1, c0:c1], in_=oslot(u - 1)), 16)
                q.wait_ge(s_out.h, s_out.n)
                q.wait_ge(s_out1.h, s_out1.n)

            @block.vector
            def _(v):
                for _h, _n in _credit:
                    if _n:
                        v.wait_ge(_h, _n)
                vfl(v)
                for u in range(16):
                    h, tb = divmod(u, NT)
                    v.wait_ge(s_bev.h, h * 16 + tb * 2 + 2)
                    v.wait_ge(x2S.ld[u % 2].h, x2S.ldb[u % 2] + 16 * (u // 2 + 1))
                    v.wait_ge(s2S.ld[u % 2].h, s2S.ldb[u % 2] + 16 * (u // 2 + 1))
                    vfl(v)
                    buf = x2S.bufs[u % 2]
                    sbuf_t = s2S.bufs[u % 2]
                    ti = v.tensor_tensor(
                        sg[:], sbuf_t[:],
                        bigB[:, tb * 2048 + h * 1024: tb * 2048 + (h + 1) * 1024],
                        ALU.mult)
                    s2S.release(ti)
                    if u >= 2:
                        v.wait_ge((s_out if u % 2 == 0 else s_out1).h,
                                  16 * (u // 2))
                    vfl(v)
                    ti = v.tensor_tensor(oslot(u), buf[:], sg[:], ALU.add)
                    s_fin.inc(ti)
                    x2S.release(v.engine_nop())

    return nc


def _stage_host(inputs):
    f = np.float32
    asf = lambda a: np.asarray(a, dtype=f)
    x = asf(inputs["L0_attn_x"]).reshape(NCORE, TPC, D)
    WqT = np.ascontiguousarray(asf(inputs["L0_attn_Wq"]).T)
    WkT = np.ascontiguousarray(asf(inputs["L0_attn_Wk"]).T)
    WvT = np.ascontiguousarray(asf(inputs["L0_attn_Wv"]).T)
    WoT = np.ascontiguousarray(asf(inputs["L0_attn_Wo"]).T)
    w1T = np.ascontiguousarray(asf(inputs["L0_ffn_w1"]).T)
    w2T = np.ascontiguousarray(asf(inputs["L0_ffn_w2"]).T)
    rT = np.ascontiguousarray(asf(inputs["L0_ffn_router_w"]).T)

    wq = np.ascontiguousarray(WqT.reshape(ND, 128, ND, 128).transpose(2, 1, 0, 3))
    wg = lambda W: np.ascontiguousarray(W.reshape(ND, 128, 4, 512).transpose(2, 0, 1, 3))
    wk, wv, wo = wg(WkT), wg(WvT), wg(WoT)
    w1 = np.ascontiguousarray(w1T.reshape(ND, 128, NF, 128).transpose(2, 1, 0, 3))
    w2 = np.ascontiguousarray(w2T.reshape(NF, 128, D))
    rt = np.ascontiguousarray(rT.reshape(ND, 128, E).transpose(1, 0, 2).reshape(128, ND * E))
    bq = np.ascontiguousarray(asf(inputs["L0_attn_bq"]).reshape(ND, 128).T)
    b1 = np.ascontiguousarray(asf(inputs["L0_ffn_b1"]).reshape(NF, 128).T)
    brows = np.ascontiguousarray(np.stack(
        [asf(inputs["L0_attn_bk"]), asf(inputs["L0_attn_bv"]),
         asf(inputs["L0_attn_bo"]), asf(inputs["L0_ffn_b2"])]))

    stairs = np.zeros((128, 4 * 512), dtype=bfloat16)
    tt = np.arange(128)[:, None]
    cc = np.arange(512)[None, :]
    for k in range(4):
        stairs[:, k * 512:(k + 1) * 512] = ((128 * k + tt) <= cc).astype(bfloat16)
    ones512 = np.ones((128, 512), dtype=bfloat16)
    ones4f = np.ones((4, 128), dtype=f)
    id8 = np.eye(8, dtype=f)
    iota = np.zeros((128, 128), dtype=f)
    iota[:, 0:64] = np.tile(np.arange(8, dtype=f), 8)[None, :]
    iota[:, 64:128] = np.tile(np.arange(8, dtype=f) - BIG, 8)[None, :]

    stri64 = np.triu(np.ones((64, 64), dtype=f), k=1)
    id64 = np.eye(64, dtype=f)
    kron8 = np.zeros((8, 64), dtype=f)
    for c in range(8):
        kron8[c, c * 8:(c + 1) * 8] = 1.0
    shared = {"wq": wq, "wk": wk, "wv": wv, "wo": wo, "w1": w1, "w2": w2,
              "rt_i": rt, "bq_i": bq, "b1_i": b1, "brows": brows,
              "stairs_i": stairs, "ones512_i": ones512, "ones4f_i": ones4f,
              "id8_i": id8, "iota_i": iota, "stri64_i": stri64, "id64_i": id64,
              "kron8_i": kron8}
    in_maps = []
    for c in range(NCORE):
        selv = np.zeros((8, 1), f)
        if c % 2 == 1:
            selv[c - 1, 0] = 1.0
        presel = np.zeros((8, 128), f)
        presel[:c, :] = 1.0
        osel = np.zeros((64, 1), f)
        osel[c * 8, 0] = 1.0
        slotrow = np.tile(c * 512.0 + np.arange(512, dtype=f), (128, 1))
        m = dict(shared)
        m.update({"x": np.ascontiguousarray(x[c]), "selc_i": selv,
                  "presel_i": presel, "aones8_i": np.ones((8, 128), f),
                  "slotrow_i": slotrow, "osel_i": osel})
        in_maps.append(m)
    return in_maps


def kernel(**inputs):
    import bass_rust as _br
    import concourse.library_config as _lc
    in_maps = _stage_host(inputs)
    nc = build_nc()
    _mask = {}
    for _lib in _lc.all_libraries:
        for _it in _lib.instructions:
            _mask[_it] = _mask.get(_it, 0) | (1 << _lib.index)
    _br.insert_library_loads(nc, _mask, len(_lc.all_libraries), _lc.standard.index)
    mybir.codegen_inst_isa_subclasses(nc)
    res = _RUN(nc, in_maps, core_ids=list(range(NCORE)))
    out = np.concatenate([r["out"] for r in res.results], axis=0)
    return out.reshape(4, 2048, 2048).astype(np.float32)


# version 10
# speedup vs baseline: 1.0631x; 1.0631x over previous
import numpy as np
import ml_dtypes
from contextlib import ExitStack

import concourse.bass as bass
import concourse.mybir as mybir
from concourse.bass_utils import run_bass_kernel_spmd

F32 = mybir.dt.float32
BF16 = mybir.dt.bfloat16
ALU = mybir.AluOpType
AF = mybir.ActivationFunctionType
AX = mybir.AxisListType
bfloat16 = ml_dtypes.bfloat16

D = 2048
ND = 16
TPC = 1024
NT = 8
FF = 8192
NF = 64
E = 8
CAP = 512.0
NCORE = 8
SCALE = float(128.0 ** -0.5)
BIG = 999.0

_RUN = run_bass_kernel_spmd


class SemC:

    def __init__(self, nc, CT, name):
        self.h = CT.enter_context(nc.semaphore(name))
        self.n = 0

    def inc(self, inst, k=1):
        inst.then_inc(self.h, k)
        self.n += k
        return inst


class Flush:

    def __init__(self, nc, CT, name, op=None):
        self.s = SemC(nc, CT, name)
        self.op = op

    def __call__(self, eng):
        inst = self.op(eng) if self.op is not None else eng.engine_nop()
        self.s.inc(inst)
        eng.wait_ge(self.s.h, self.s.n)


class SemPool:

    def __init__(self, nc, CT, name, nslots):
        self.ld = [SemC(nc, CT, f"{name}_ld{i}") for i in range(nslots)]
        self.fr = [SemC(nc, CT, f"{name}_fr{i}") for i in range(nslots)]
        self.nmax = nslots


class Stream2:

    def __init__(self, nc, CT, PS, name, shape, dtype, nslots=2, side="right", pool=None):
        self.bufs = [PS.enter_context(
            nc.sbuf_tensor(f"{name}{i}", list(shape), dtype, side=side))
            for i in range(nslots)]
        if pool is None:
            pool = SemPool(nc, CT, name, nslots)
        assert nslots <= pool.nmax
        self.ld = pool.ld[:nslots]
        self.fr = pool.fr[:nslots]
        self.ldb = [x.n for x in self.ld]
        self.frb = [x.n for x in self.fr]
        self.n = nslots
        self.ni = 0
        self.na = 0
        self.nc = 0

    def load(self, q, in_ap, transpose=False):
        i = self.ni
        s = i % self.n
        if i >= self.n:
            q.wait_ge(self.fr[s].h, self.frb[s] + i // self.n)
        elif self.frb[s] > 0:
            q.wait_ge(self.fr[s].h, self.frb[s])
            q.wait_ge(self.ld[s].h, self.ldb[s])
        kw = {"transpose": True} if transpose else {}
        q.dma_start(out=self.bufs[s][:], in_=in_ap, **kw).then_inc(self.ld[s].h, 16)
        self.ld[s].n += 16
        self.ni += 1

    def acquire(self, eng):
        i = self.na
        s = i % self.n
        eng.wait_ge(self.ld[s].h, self.ldb[s] + 16 * (i // self.n + 1))
        self.na += 1
        return self.bufs[s]

    def release(self, inst):
        s = self.nc % self.n
        inst.then_inc(self.fr[s].h, 1)
        self.fr[s].n += 1
        self.nc += 1

    def release_skip(self):
        self.fr[self.nc % self.n].n += 1
        self.nc += 1


def build_nc():
    nc = bass.Bass("TRN2", target_bir_lowering=False)
    dt_in = lambda n, s: nc.dram_tensor(n, list(s), F32, kind="ExternalInput").ap()
    dt_in16 = lambda n, s: nc.dram_tensor(n, list(s), BF16, kind="ExternalInput").ap()

    x_d = dt_in("x", [TPC, D])
    wq_d = dt_in("wq", [ND, 128, ND, 128])
    wk_d = dt_in("wk", [4, ND, 128, 512])
    wv_d = dt_in("wv", [4, ND, 128, 512])
    wo_d = dt_in("wo", [4, ND, 128, 512])
    w1_d = dt_in("w1", [NF, 128, ND, 128])
    w2_d = dt_in("w2", [NF, 128, D])
    rt_d = dt_in("rt_i", [128, ND * E])
    bq_d = dt_in("bq_i", [128, ND])
    b1_d = dt_in("b1_i", [128, NF])
    brows_d = dt_in("brows", [4, D])
    stairs_d = dt_in16("stairs_i", [128, 4 * 512])
    ones512_d = dt_in16("ones512_i", [128, 512])
    ones4f_d = dt_in("ones4f_i", [4, 128])
    id8_d = dt_in("id8_i", [8, 8])
    iota_d = dt_in("iota_i", [128, 128])
    selc_d = dt_in("selc_i", [8, 1])
    presel_d = dt_in("presel_i", [8, 128])
    aones8_d = dt_in("aones8_i", [8, 128])
    slotrow_d = dt_in("slotrow_i", [128, 512])
    stri64_d = dt_in("stri64_i", [64, 64])
    id64_d = dt_in("id64_i", [64, 64])
    osel_d = dt_in("osel_i", [64, 1])
    kron8_d = dt_in("kron8_i", [8, 64])

    out_d = nc.dram_tensor("out", [TPC, D], F32, kind="ExternalOutput").ap()
    xnh_d = nc.dram_tensor("xnh_s", [TPC, D], BF16).ap()
    xnl_d = nc.dram_tensor("xnl_s", [TPC, D], BF16).ap()
    s_d = nc.dram_tensor("s_scr", [TPC, D], BF16).ap()
    CCW = 2 * E + D + 4 * TPC
    cc_in = nc.dram_tensor("cc_in", [1, CCW], F32)
    cc_out = nc.dram_tensor("cc_out", [NCORE, CCW], F32)
    ccx_out = nc.dram_tensor("ccx_out", [NCORE * TPC, D], BF16, addr_space="Shared")
    cc3_in = [nc.dram_tensor(f"cc3_in{h}", [512, D // 2], BF16) for h in range(2)]
    cc3_out = [nc.dram_tensor(f"cc3_out{h}", [4096, D // 2], BF16, addr_space="Shared")
               for h in range(2)]
    scr_t = nc.dram_tensor("scr_t", [1, 512], F32).ap()
    scr_r = nc.dram_tensor("scr_r", [128, 8], F32).ap()

    with ExitStack() as CT:
        EC = CT.enter_context
        stairs = EC(nc.sbuf_tensor("stairs", [128, 2048], BF16))
        ones512 = EC(nc.sbuf_tensor("ones512", [128, 512], BF16))
        ones4f = EC(nc.sbuf_tensor("ones4f", [4, 128], F32))
        id8 = EC(nc.sbuf_tensor("id8", [8, 8], F32))
        iota2 = EC(nc.sbuf_tensor("iota2", [128, 128], F32))
        selc = EC(nc.sbuf_tensor("selc", [8, 1], F32))
        presel = EC(nc.sbuf_tensor("presel", [8, 128], F32))
        aones8 = EC(nc.sbuf_tensor("aones8", [8, 128], F32))
        stri64 = EC(nc.sbuf_tensor("stri64", [64, 64], F32))
        id64 = EC(nc.sbuf_tensor("id64", [64, 64], F32))
        osel = EC(nc.sbuf_tensor("osel", [64, 1], F32))
        kron8 = EC(nc.sbuf_tensor("kron8", [8, 64], F32))
        otab = EC(nc.sbuf_tensor("otab", [8, 16], F32))
        orow = EC(nc.sbuf_tensor("orow", [1, 64], F32))
        eq64 = EC(nc.sbuf_tensor("eq64", [128, 64], F32))
        idxa = EC(nc.sbuf_tensor("idxa", [128, 8], F32))
        idxb = EC(nc.sbuf_tensor("idxb", [128, 8], F32))
        kb64 = EC(nc.sbuf_tensor("kb64", [128, 64], BF16))
        P64 = EC(nc.sbuf_tensor("P64", [128, 64], F32))
        g64a = EC(nc.sbuf_tensor("g64a", [128, 64], F32))
        g64b = EC(nc.sbuf_tensor("g64b", [128, 64], F32))
        g64c = EC(nc.sbuf_tensor("g64c", [128, 64], F32))
        t64d = EC(nc.sbuf_tensor("t64d", [128, 64], F32))
        wrk64 = EC(nc.sbuf_tensor("wrk64", [128, 64], F32))
        exr = EC(nc.sbuf_tensor("exr", [64, 2], F32))
        oor = EC(nc.sbuf_tensor("oor", [1, 2], F32))
        oocol = EC(nc.sbuf_tensor("oocol", [128, 1], F32))
        ro8 = EC(nc.sbuf_tensor("ro8", [128, 8], F32))
        idq = EC(nc.sbuf_tensor("idq", [128, 32], mybir.dt.int16))
        ido = EC(nc.sbuf_tensor("ido", [128, 64], mybir.dt.int16))
        idqf = EC(nc.sbuf_tensor("idqf", [32, 32], F32))
        idof = EC(nc.sbuf_tensor("idof", [32, 64], F32))
        tokrow = EC(nc.sbuf_tensor("tokrow", [1, 512], F32))
        rt = EC(nc.sbuf_tensor("rt", [128, 128], F32))
        rthi = EC(nc.sbuf_tensor("rthi", [128, 128], BF16))
        rtlo = EC(nc.sbuf_tensor("rtlo", [128, 128], BF16))
        bqc = EC(nc.sbuf_tensor("bqc", [128, 16], F32))
        b1c = EC(nc.sbuf_tensor("b1c", [128, 64], F32))
        z_sb = EC(nc.sbuf_tensor("z_sb", [128, 64], F32))
        oh12 = EC(nc.sbuf_tensor("oh12", [128, 128], BF16))
        small = EC(nc.sbuf_tensor("small", [128, 372], F32))
        junk8 = small[:, 0:8]
        p1c = small[:, 8:16]
        p2c = small[:, 16:24]
        c1s = small[:, 24:32]
        c2s = small[:, 32:40]
        w_sb = small[:, 40:48]
        off1 = small[:, 48:56]
        off2 = small[:, 56:64]
        ssel = small[:, 64:80]
        tcol = [small[:, 80 + i:81 + i] for i in range(4)]
        epsc = small[:, 84:85]
        flcell = small[:, 85:86]
        junkB = small[:, 86:94]
        flcellB = small[:, 94:95]
        off1_64 = small[:, 96:160]
        off2_64 = small[:, 160:224]
        w64a = small[:, 224:288]
        w64b = small[:, 288:352]
        a8 = small[:, 352:360]
        b8 = small[:, 360:368]
        flc4 = [small[:, 368 + i:369 + i] for i in range(4)]
        n_sb = EC(nc.sbuf_tensor("n_sb", [1, 16], F32))
        gn_sb = EC(nc.sbuf_tensor("gn_sb", [8, 16], F32))
        bigA = EC(nc.sbuf_tensor("bigA", [128, 32768], BF16))
        bigB = EC(nc.sbuf_tensor("bigB", [128, 16384], BF16))
        xnT = EC(nc.sbuf_tensor("xnT", [128, 16384], BF16))
        PHIKV, PHIQ = 0, 16384

        pb = [EC(nc.psum_tensor(f"pb{i}", [128, 512], F32)) for i in range(8)]

        S = lambda name: SemC(nc, CT, name)
        s_k = S("s_k")
        s_lnv = S("s_lnv")
        s_lncp = S("s_lncp")
        s_lnsq = S("s_lnsq")
        s_lnb = S("s_lnb")
        s_lnr = S("s_lnr")
        s_xw = S("s_xw")
        s_bkv = S("s_bkv")
        s_bo = S("s_bo")
        s_b2 = S("s_b2")
        s_xt = S("s_xt")
        s_rt = S("s_rt")
        s_kmm = S("s_kmm")
        s_kev = S("s_kev")
        s_stt = S("s_stt")
        s_vmm = S("s_vmm")
        s_vev = S("s_vev")
        s_zmm = S("s_zmm")
        s_zev = S("s_zev")
        s_ztr = S("s_ztr")
        s_zte = S("s_zte")
        s_smm = S("s_smm")
        s_sev = S("s_sev")
        s_oh = S("s_oh")
        s_cmm = S("s_cmm")
        s_cev = S("s_cev")
        s_nmm = S("s_nmm")
        s_nev = S("s_nev")
        s_pub = S("s_pub")
        s_cc = S("s_cc")
        s_g = S("s_g")
        s_qmm = S("s_qmm")
        s_qev = S("s_qev")
        s_qstt = S("s_qstt")
        s_x7a = S("s_x7a")
        s_x7b = S("s_x7b")
        s_x7c = S("s_x7c")
        s_x7d = S("s_x7d")
        s_wch = S("s_wch")
        s_umm = S("s_umm")
        s_uev = S("s_uev")
        s_omm = S("s_omm")
        s_oev = S("s_oev")
        s_std = S("s_std0")
        s_std1 = S("s_std1")
        s_amm = S("s_amm")
        s_aev = S("s_aev")
        s_bmm = S("s_bmm")
        s_bwf = S("s_bwf")
        s_bev = S("s_bev")
        s_fin = S("s_fin")
        s_out = S("s_out")
        s_out1 = S("s_out1")
        s_ccx = S("s_ccx")
        s_tb = S("s_tb")
        s_kb = S("s_kb")
        s_ob = S("s_ob")
        s_obe = S("s_obe")
        s_pgm = S("s_pgm")
        s_pge = S("s_pge")
        s_exc = S("s_exc")
        s_cmpv = S("s_cmpv")
        s_cmpm = S("s_cmpm")
        s_tokm = S("s_tokm")
        s_toke = S("s_toke")
        s_owv = S("s_owv")
        s_bnc = S("s_bnc")
        s_idx = S("s_idx")
        s_xg = S("s_xg")
        s_fpv = S("s_fpv")
        s_fpd = S("s_fpd")
        s_fpd1 = S("s_fpd1")
        s_cc3 = S("s_cc3")
        s_cc3b = S("s_cc3b")
        s_fg = S("s_fg")
        s_fgb = S("s_fgb")
        s_gw = S("s_gw")
        poolG = SemPool(nc, CT, "pG", 4)
        poolS = SemPool(nc, CT, "pS", 2)
        poolT = SemPool(nc, CT, "pT", 2)
        poolU = SemPool(nc, CT, "pU", 2)
        vfl = Flush(nc, CT, "vfl")
        sfl = Flush(nc, CT, "sfl", op=lambda eng: eng.copy(flcell, epsc))

        block = EC(nc.Block())

        ones128 = ones512[:, 0:128]
        triu01 = stairs[:, 0:128]
        onescol = ones512[:, 0:1]

        def xslice(m, tb):
            return xnT[:, m * 1024 + tb * 128: m * 1024 + (tb + 1) * 128]

        def xhalf(m, th):
            return xnT[:, m * 1024 + th * 512: m * 1024 + (th + 1) * 512]

        def G8v(t):
            return t.rearrange("p (g e) -> p g e", e=8)

        def C8v(t):
            return t.rearrange("p (g o) -> p g o", o=1)

        def B8v(t):
            return C8v(t).broadcast_to([128, 8, 8])

        @block.sync
        def _(q):
            for dst, src in ((stairs, stairs_d), (ones512, ones512_d),
                             (ones4f, ones4f_d), (id8, id8_d), (iota2, iota_d),
                             (selc, selc_d), (presel, presel_d), (aones8, aones8_d),
                             (rt, rt_d), (bqc, bq_d), (b1c, b1_d),
                             (stri64, stri64_d), (kron8, kron8_d),
                             (id64, id64_d), (osel, osel_d)):
                s_k.inc(q.dma_start(out=dst[:], in_=src), 16)

        @block.vector
        def _(v):
            v.wait_ge(s_k.h, s_k.n)
            s_rt.inc(v.memset(epsc, 1e-5))
            s_rt.inc(v.tensor_copy(rthi[:], rt[:]))
            vfl(v)
            s_rt.inc(v.tensor_tensor(rtlo[:], rt[:], rthi[:], ALU.subtract))

        with ExitStack() as PO:
            gs_sb = PO.enter_context(nc.sbuf_tensor("gsS", [8, D], F32, side="right"))
            srow = PO.enter_context(nc.sbuf_tensor("srow", [1, D], F32, side="right"))

            with ExitStack() as PS:
                xc = PS.enter_context(nc.sbuf_tensor("xc", [128, D], F32, side="right"))
                xh1 = PS.enter_context(nc.sbuf_tensor("xh1", [128, D], BF16, side="right"))
                xl1 = PS.enter_context(nc.sbuf_tensor("xl1", [128, D], BF16, side="right"))
                lncol = PS.enter_context(nc.sbuf_tensor("lncol", [128, 8], F32, side="right"))
                xS = Stream2(nc, CT, PS, "xS", [128, D], F32, nslots=2, pool=poolS)
                sums, mu, ssq, t1, t2, varr, srt, rinv = (lncol[:, i:i + 1] for i in range(8))

                @block.sync
                def _(q):
                    for j in range(NT):
                        xS.load(q, x_d[j * 128:(j + 1) * 128, :])
                        if j >= 1:
                            q.wait_ge(s_lnv.h, j)
                            s_xw.inc(q.dma_start(out=xnh_d[(j - 1) * 128:j * 128, :], in_=xh1[:]), 16)
                            s_xw.inc(q.dma_start(out=xnl_d[(j - 1) * 128:j * 128, :], in_=xl1[:]), 16)
                    q.wait_ge(s_lnv.h, NT)
                    s_xw.inc(q.dma_start(out=xnh_d[(NT - 1) * 128:NT * 128, :], in_=xh1[:]), 16)
                    s_xw.inc(q.dma_start(out=xnl_d[(NT - 1) * 128:NT * 128, :], in_=xl1[:]), 16)
                    q.wait_ge(s_xw.h, s_xw.n)
                    for m in range(ND):
                        s_xt.inc(q.dma_start(
                            out=xnT[:, m * 1024:(m + 1) * 1024],
                            in_=xnh_d[0:TPC, m * 128:(m + 1) * 128],
                            transpose=True), 16)

                @block.scalar
                def _(sc):
                    sc.wait_ge(s_rt.h, 1)
                    for j in range(NT):
                        sc.wait_ge(xS.ld[j % 2].h, xS.ldb[j % 2] + 16 * (j // 2 + 1))
                        if j >= 1:
                            sc.wait_ge(s_lnv.h, j)
                        buf = xS.bufs[j % 2]
                        s_lncp.inc(sc.activation(xc[:], buf[:], AF.Copy, accum_out=sums))
                        sfl(sc)
                        s_lnsq.inc(sc.activation(xc[:], buf[:], AF.Square, accum_out=ssq))
                        sc.wait_ge(s_lnr.h, j + 1)
                        s_lnb.inc(sc.activation(srt, varr, AF.Sqrt, bias=epsc, scale=1.0))

                @block.vector
                def _(v):
                    inv_d = 1.0 / D
                    for j in range(NT):
                        v.wait_ge(xS.ld[j % 2].h, xS.ldb[j % 2] + 16 * (j // 2 + 1))
                        buf = xS.bufs[j % 2]
                        v.wait_ge(s_lncp.h, j + 1)
                        v.tensor_scalar_mul(mu, sums, inv_d)
                        v.wait_ge(s_lnsq.h, j + 1)
                        v.tensor_scalar_mul(t1, ssq, inv_d)
                        vfl(v)
                        v.tensor_mul(t2, mu, mu)
                        vfl(v)
                        s_lnr.inc(v.tensor_sub(varr, t1, t2))
                        v.wait_ge(s_lnb.h, j + 1)
                        v.reciprocal(rinv, srt)
                        if j >= 1:
                            v.wait_ge(s_xw.h, 32 * j)
                        vfl(v)
                        v.tensor_scalar(xh1[:], buf[:], mu, rinv, ALU.subtract, ALU.mult)
                        ti = v.tensor_scalar(xc[:], buf[:], mu, rinv, ALU.subtract, ALU.mult)
                        xS.release(ti)
                        vfl(v)
                        s_lnv.inc(v.tensor_tensor(xl1[:], xc[:], xh1[:], ALU.subtract))

            with ExitStack() as PS:
                phik = PS.enter_context(nc.sbuf_tensor("phik", [128, 8 * 512], BF16, side="right"))
                ekt = PS.enter_context(nc.sbuf_tensor("ekt", [128, 1024], F32, side="right"))
                rkt = PS.enter_context(nc.sbuf_tensor("rkt", [128, 1024], F32, side="right"))
                bk_r = PS.enter_context(nc.sbuf_tensor("bk_r", [1, D], BF16, side="right"))
                bv_r = PS.enter_context(nc.sbuf_tensor("bv_r", [1, D], BF16, side="right"))
                wkvS = Stream2(nc, CT, PS, "wkvS", [128, 512], BF16, nslots=4, pool=poolG)
                eslot = lambda t, u: t[:, (u % 2) * 512:((u % 2) + 1) * 512]

                @block.gpsimd
                def _(g):
                    g.wait_ge(s_lnv.h, 8)
                    g.wait_ge(s_lnsq.h, 8)
                    g.wait_ge(s_xw.h, 256)
                    s_bkv.inc(g.dma_start(out=bk_r[:], in_=brows_d[0:1, :]), 16)
                    s_bkv.inc(g.dma_start(out=bv_r[:], in_=brows_d[1:2, :]), 16)
                    for eg in range(4):
                        for m in range(ND):
                            wkvS.load(g, wk_d[eg, m])
                        for m in range(ND):
                            wkvS.load(g, wv_d[eg, m])
                    s_ccx.inc(g.collective_compute(
                        "AllGather", ALU.bypass,
                        replica_groups=[list(range(NCORE))],
                        ins=[xnh_d.opt()],
                        outs=[ccx_out.ap().opt()]))

                @block.tensor
                def _(t):
                    t.wait_ge(s_xt.h, 256)
                    t.wait_ge(s_k.h, s_k.n)
                    t.wait_ge(s_bkv.h, 32)
                    for eg in range(4):
                        if eg > 0:
                            t.wait_ge(s_vev.h, eg * 8)
                        for m in range(ND):
                            buf = wkvS.acquire(t)
                            for tb in range(NT):
                                ti = t.matmul(pb[tb][:], xslice(m, tb), buf[:],
                                              start=(m == 0), stop=False)
                            wkvS.release(ti)
                        for tb in range(NT):
                            ti = t.matmul(pb[tb][:], ones512[0:1, 0:128],
                                          bk_r[0:1, eg * 512:(eg + 1) * 512],
                                          start=False, stop=True)
                            s_kmm.inc(ti)
                        for m in range(ND):
                            buf = wkvS.acquire(t)
                            for tb in range(NT):
                                if m == 0:
                                    t.wait_ge(s_kev.h, eg * 8 + tb + 1)
                                ti = t.matmul(pb[tb][:], xslice(m, tb), buf[:],
                                              start=(m == 0), stop=False)
                            wkvS.release(ti)
                        for tb in range(NT):
                            ti = t.matmul(pb[tb][:], ones512[0:1, 0:128],
                                          bv_r[0:1, eg * 512:(eg + 1) * 512],
                                          start=False, stop=True)
                            s_vmm.inc(ti)

                @block.scalar
                def _(sc):
                    sc.wait_ge(s_rt.h, 1)
                    sfl(sc)
                    for eg in range(4):
                        for tb in range(NT):
                            u = eg * 8 + tb
                            sc.wait_ge(s_kmm.h, u + 1)
                            if u >= 2:
                                sc.wait_ge(s_stt.h, u - 1)
                            sc.activation(eslot(ekt, u), pb[tb][:], AF.Exp)
                            s_kev.inc(sc.activation(eslot(rkt, u), pb[tb][:], AF.Relu))

                @block.vector
                def _(v):
                    for eg in range(4):
                        vfl(v)
                        for tb in range(NT):
                            u = eg * 8 + tb
                            v.wait_ge(s_kev.h, u + 1)
                            s_stt.inc(v.scalar_tensor_tensor(
                                phik[:, tb * 512:(tb + 1) * 512],
                                eslot(ekt, u), 1.0, eslot(rkt, u), ALU.min, ALU.add))
                        vfl(v)
                        for tb in range(NT):
                            u = eg * 8 + tb
                            v.wait_ge(s_vmm.h, u + 1)
                            ti = v.tensor_tensor(
                                bigA[:, PHIKV + tb * 2048 + eg * 512: PHIKV + tb * 2048 + (eg + 1) * 512],
                                phik[:, tb * 512:(tb + 1) * 512], pb[tb][:], ALU.mult)
                            s_vev.inc(ti)

            with ExitStack() as PS:
                zfm = PS.enter_context(nc.sbuf_tensor("zfm", [8, 1024], F32, side="right"))
                xlS = Stream2(nc, CT, PS, "xlS", [128, 1024], BF16, nslots=2, pool=poolS)

                @block.sync
                def _(q):
                    q.wait_ge(s_vev.h, 32)
                    q.wait_ge(s_stt.h, 32)
                    for m in range(ND):
                        xlS.load(q, xnl_d[0:TPC, m * 128:(m + 1) * 128], transpose=True)

                @block.tensor
                def _(t):
                    t.wait_ge(s_vev.h, 32)
                    t.wait_ge(s_rt.h, 3)
                    ti = None
                    for m in range(ND):
                        buf = xlS.acquire(t)
                        for th in range(2):
                            st = (m == 0)
                            sp = (m == ND - 1)
                            zp = pb[th][0:8, :]
                            t.matmul(zp, rthi[:, m * 8:(m + 1) * 8], xhalf(m, th), start=st, stop=False)
                            t.matmul(zp, rtlo[:, m * 8:(m + 1) * 8], xhalf(m, th), start=False, stop=False)
                            ti = t.matmul(zp, rthi[:, m * 8:(m + 1) * 8],
                                          buf[:, th * 512:(th + 1) * 512], start=False, stop=sp)
                            if sp:
                                s_zmm.inc(ti)
                        if m < ND - 1:
                            xlS.release(ti)
                    for tb in range(NT):
                        t.wait_ge(s_zev.h, 2)
                        if tb >= 1:
                            t.wait_ge(s_zte.h, tb)
                        s_ztr.inc(t.transpose(pb[2][0:128, 0:8],
                                              zfm[0:8, tb * 128:(tb + 1) * 128], id8[:]))

                @block.vector
                def _(v):
                    for th in range(2):
                        v.wait_ge(s_zmm.h, th + 1)
                        s_zev.inc(v.tensor_copy(zfm[0:8, th * 512:(th + 1) * 512], pb[th][0:8, :]))
                    xlS.release(v.engine_nop())
                    for tb in range(NT):
                        v.wait_ge(s_ztr.h, tb + 1)
                        s_zte.inc(v.tensor_copy(z_sb[:, tb * 8:(tb + 1) * 8], pb[2][0:128, 0:8]))

            @block.tensor
            def _(t):
                for g2 in range(4):
                    bank = pb[4 + g2 % 2]
                    if g2 >= 2:
                        t.wait_ge(s_sev.h, g2 - 1)
                    for tb in range(NT):
                        ti = t.matmul(bank[0:1, :], onescol,
                                      bigA[:, PHIKV + tb * 2048 + g2 * 512: PHIKV + tb * 2048 + (g2 + 1) * 512],
                                      start=(tb == 0), stop=(tb == NT - 1))
                    s_smm.inc(ti)

            @block.vector
            def _(v):
                for g2 in range(4):
                    v.wait_ge(s_smm.h, g2 + 1)
                    s_sev.inc(v.tensor_copy(srow[0:1, g2 * 512:(g2 + 1) * 512],
                                            pb[4 + g2 % 2][0:1, :]))

            with ExitStack() as PS:
                r64a = PS.enter_context(nc.sbuf_tensor("r64a", [128, 64], F32, side="right"))
                r64b = PS.enter_context(nc.sbuf_tensor("r64b", [128, 64], F32, side="right"))
                r64c = PS.enter_context(nc.sbuf_tensor("r64c", [128, 64], F32, side="right"))
                rm8a = PS.enter_context(nc.sbuf_tensor("rm8a", [128, 8], F32, side="right"))
                rm8b = PS.enter_context(nc.sbuf_tensor("rm8b", [128, 8], F32, side="right"))
                iota64 = iota2[:, 0:64]
                iotaC64 = iota2[:, 64:128]
                oh1_64 = oh12[:, 0:64]
                oh2_64 = oh12[:, 64:128]

                @block.vector
                def _(v):
                    v.wait_ge(s_zte.h, 8)
                    vfl(v)
                    v.reduce_max(C8v(rm8a[:]), G8v(z_sb[:]), axis=AX.X)
                    vfl(v)
                    s_oh.inc(v.tensor_tensor(G8v(r64a[:]), G8v(z_sb[:]), B8v(rm8a[:]), ALU.subtract))

                @block.scalar
                def _(sc):
                    sfl(sc)
                    sc.wait_ge(s_oh.h, 1)
                    s_oh.inc(sc.activation(r64c[:], r64a[:], AF.Exp))

                @block.vector
                def _(v):
                    v.wait_ge(s_oh.h, 2)
                    v.reduce_sum(C8v(rm8a[:]), G8v(r64c[:]), axis=AX.X)
                    vfl(v)
                    v.reciprocal(rm8b[:], rm8a[:])
                    vfl(v)
                    v.tensor_tensor(G8v(r64b[:]), G8v(r64c[:]), B8v(rm8b[:]), ALU.mult)
                    vfl(v)
                    v.reduce_max(C8v(rm8a[:]), G8v(r64b[:]), axis=AX.X)
                    vfl(v)
                    v.tensor_tensor(G8v(r64a[:]), G8v(r64b[:]), B8v(rm8a[:]), ALU.is_ge)
                    vfl(v)
                    v.tensor_tensor(r64a[:], r64a[:], iotaC64, ALU.mult)
                    vfl(v)
                    v.tensor_single_scalar(r64a[:], r64a[:], BIG, ALU.add)
                    vfl(v)
                    v.tensor_reduce(C8v(idxa[:]), G8v(r64a[:]), AX.X, ALU.min)
                    vfl(v)
                    v.tensor_tensor(G8v(oh1_64), G8v(iota64), B8v(idxa[:]), ALU.is_equal)
                    vfl(v)
                    v.tensor_tensor(r64a[:], r64b[:], oh1_64, ALU.mult)
                    vfl(v)
                    v.reduce_sum(C8v(p1c[:]), G8v(r64a[:]), axis=AX.X)
                    v.scalar_tensor_tensor(r64c[:], oh1_64, -10000.0, r64b[:], ALU.mult, ALU.add)
                    vfl(v)
                    v.reduce_max(C8v(rm8a[:]), G8v(r64c[:]), axis=AX.X)
                    vfl(v)
                    v.tensor_tensor(G8v(r64a[:]), G8v(r64c[:]), B8v(rm8a[:]), ALU.is_ge)
                    vfl(v)
                    v.tensor_tensor(r64a[:], r64a[:], iotaC64, ALU.mult)
                    vfl(v)
                    v.tensor_single_scalar(r64a[:], r64a[:], BIG, ALU.add)
                    vfl(v)
                    v.tensor_reduce(C8v(idxb[:]), G8v(r64a[:]), AX.X, ALU.min)
                    vfl(v)
                    v.tensor_tensor(G8v(oh2_64), G8v(iota64), B8v(idxb[:]), ALU.is_equal)
                    vfl(v)
                    v.tensor_tensor(r64a[:], r64b[:], oh2_64, ALU.mult)
                    vfl(v)
                    s_oh.inc(v.reduce_sum(C8v(p2c[:]), G8v(r64a[:]), axis=AX.X))

                @block.tensor
                def _(t):
                    t.wait_ge(s_zte.h, 8)
                    t.wait_ge(s_oh.h, 3)
                    t.wait_ge(s_sev.h, 4)
                    for tbp in range(NT):
                        b1k = pb[6 + tbp % 2]
                        b2k = pb[4 + tbp % 2]
                        if tbp >= 2:
                            t.wait_ge(s_cev.h, tbp - 1)
                        for j in range(tbp + 1):
                            lhs = ones128 if j < tbp else triu01
                            t.matmul(b1k[:, 0:8], lhs, oh1_64[:, j * 8:(j + 1) * 8],
                                     start=(j == 0), stop=(j == tbp))
                            ti = t.matmul(b2k[:, 0:8], lhs, oh2_64[:, j * 8:(j + 1) * 8],
                                          start=(j == 0), stop=(j == tbp))
                        s_cmm.inc(ti)
                    t.wait_ge(s_cev.h, 8)
                    for tb in range(NT):
                        t.matmul(pb[4][0:1, 0:8], onescol, oh1_64[:, tb * 8:(tb + 1) * 8],
                                 start=(tb == 0), stop=(tb == NT - 1))
                        ti = t.matmul(pb[5][0:1, 0:8], onescol, oh2_64[:, tb * 8:(tb + 1) * 8],
                                      start=(tb == 0), stop=(tb == NT - 1))
                    s_nmm.inc(ti)

                @block.vector
                def _(v):
                    for tbp in range(NT):
                        b1k = pb[6 + tbp % 2]
                        b2k = pb[4 + tbp % 2]
                        v.wait_ge(s_cmm.h, tbp + 1)
                        vfl(v)
                        v.tensor_tensor(junk8, b1k[:, 0:8], oh1_64[:, tbp * 8:(tbp + 1) * 8], ALU.mult)
                        v.tensor_tensor(junkB, b2k[:, 0:8], oh2_64[:, tbp * 8:(tbp + 1) * 8], ALU.mult)
                        vfl(v)
                        v.reduce_sum(c1s[:, tbp:tbp + 1], junk8, axis=AX.X)
                        ti = v.reduce_sum(c2s[:, tbp:tbp + 1], junkB, axis=AX.X)
                        s_cev.inc(ti)
                    v.wait_ge(s_nmm.h, 1)
                    vfl(v)
                    v.tensor_copy(n_sb[0:1, 0:8], pb[4][0:1, 0:8])
                    s_nev.inc(v.tensor_copy(n_sb[0:1, 8:16], pb[5][0:1, 0:8]))

            @block.sync
            def _(q):
                q.wait_ge(s_sev.h, 4)
                q.wait_ge(s_nev.h, 1)
                s_pub.inc(q.dma_start(out=cc_in.ap()[0:1, 0:16], in_=n_sb[:]), 16)
                s_pub.inc(q.dma_start(out=cc_in.ap()[0:1, 16:16 + D], in_=srow[:]), 16)
                q.wait_ge(s_oh.h, 3)
                q.wait_ge(s_cev.h, 8)
                base = 16 + D
                for k, tsrc in enumerate((idxa, idxb, c1s, c2s)):
                    s_pub.inc(q.dma_start(
                        out=cc_in.ap()[0:1, base + k * TPC: base + (k + 1) * TPC]
                           .rearrange("o (p a) -> (o p) a", a=8),
                        in_=tsrc[:] if k < 2 else tsrc), 16)

            @block.gpsimd
            def _(g):
                g.wait_ge(s_pub.h, 96)
                s_cc.inc(g.collective_compute(
                    "AllGather", ALU.bypass,
                    replica_groups=[list(range(NCORE))],
                    ins=[cc_in.ap().opt()],
                    outs=[cc_out.ap().opt()]))

            @block.sync
            def _(q):
                q.wait_ge(s_cc.h, 1)
                s_g.inc(q.dma_start(out=gn_sb[:], in_=cc_out.ap()[:, 0:16]), 16)
                s_g.inc(q.dma_start(out=gs_sb[:], in_=cc_out.ap()[:, 16:16 + D]), 16)
                base = 16 + D
                for k, tdst in enumerate((g64a, g64b, g64c, t64d)):
                    s_tb.inc(q.dma_start(
                        out=tdst[:].rearrange("p (c a) -> p c a", a=8),
                        in_=cc_out.ap()[:, base + k * TPC: base + (k + 1) * TPC]
                            .rearrange("c (p a) -> p c a", p=128),
                        allow_slow_non_contiguous=True), 16)

            with ExitStack() as PS:
                eqt = PS.enter_context(nc.sbuf_tensor("eqt", [128, 2048], F32, side="right"))
                rqt = PS.enter_context(nc.sbuf_tensor("rqt", [128, 2048], F32, side="right"))
                wqS = Stream2(nc, CT, PS, "wqS", [128, 2048], BF16, nslots=3, pool=poolG)
                eslot = lambda t, u: t[:, (u % 4) * 512:((u % 4) + 1) * 512]

                @block.gpsimd
                def _(g):
                    g.wait_ge(s_oh.h, 3)
                    for ec in range(ND):
                        wqS.load(g, wq_d[ec])

                @block.tensor
                def _(t):
                    t.wait_ge(s_zev.h, 2)
                    for ec in range(ND):
                        buf = wqS.acquire(t)
                        for th in range(2):
                            u = ec * 2 + th
                            if u >= 4:
                                t.wait_ge(s_qev.h, u - 3)
                            bank = pb[u % 4]
                            for m in range(ND):
                                ti = t.matmul(bank[:], buf[:, m * 128:(m + 1) * 128],
                                              xhalf(m, th), start=(m == 0), stop=(m == ND - 1))
                            s_qmm.inc(ti)

                @block.scalar
                def _(sc):
                    sfl(sc)
                    for ec in range(ND):
                        for th in range(2):
                            u = ec * 2 + th
                            sc.wait_ge(s_qmm.h, u + 1)
                            if u >= 4:
                                sc.wait_ge(s_qstt.h, u - 3)
                            bank = pb[u % 4]
                            ei = sc.activation(eslot(eqt, u), bank[:], AF.Exp,
                                               bias=bqc[:, ec:ec + 1], scale=1.0)
                            if th == 1:
                                wqS.release(ei)
                            s_qev.inc(sc.activation(eslot(rqt, u), bank[:], AF.Relu,
                                                    bias=bqc[:, ec:ec + 1], scale=1.0))

                @block.vector
                def _(v):
                    for ec in range(ND):
                        for th in range(2):
                            u = ec * 2 + th
                            v.wait_ge(s_qev.h, u + 1)
                            ti = v.scalar_tensor_tensor(
                                bigA[:, PHIQ + ec * 1024 + th * 512: PHIQ + ec * 1024 + (th + 1) * 512],
                                eslot(eqt, u), 1.0, eslot(rqt, u), ALU.min, ALU.add)
                            s_qstt.inc(ti)

            @block.tensor
            def _(t):
                t.wait_ge(s_g.h, 32)
                t.wait_ge(s_qev.h, 32)
                for m in range(ND):
                    ti = t.matmul(pb[2][:, m:m + 1], gs_sb[0:8, m * 128:(m + 1) * 128],
                                  selc[:], start=True, stop=True)
                s_x7a.inc(ti)

            @block.vector
            def _(v):
                v.wait_ge(s_x7a.h, 1)
                s_x7b.inc(v.tensor_copy(ssel, pb[2][:, 0:16]))

            @block.tensor
            def _(t):
                t.wait_ge(s_x7b.h, 1)
                for g in range(8):
                    ti = t.matmul(pb[2][:, 64 + g * 8: 72 + g * 8], presel[:], gn_sb[0:8, 0:8],
                                  start=True, stop=True)
                for g in range(8):
                    t.matmul(pb[3][:, g * 8:(g + 1) * 8], aones8[:], gn_sb[0:8, 0:8],
                             start=True, stop=False)
                    ti = t.matmul(pb[3][:, g * 8:(g + 1) * 8], presel[:], gn_sb[0:8, 8:16],
                                  start=False, stop=True)
                s_x7c.inc(ti)

            @block.vector
            def _(v):
                v.wait_ge(s_x7c.h, 1)
                v.tensor_copy(off1_64, pb[2][:, 64:128])
                s_x7d.inc(v.tensor_copy(off2_64, pb[3][:, 0:64]))
                vfl(v)
                v.tensor_tensor(w64a, off1_64, oh12[:, 0:64], ALU.mult)
                v.tensor_tensor(w64b, off2_64, oh12[:, 64:128], ALU.mult)
                vfl(v)
                v.reduce_sum(C8v(a8), G8v(w64a), axis=AX.X)
                v.reduce_sum(C8v(b8), G8v(w64b), axis=AX.X)
                vfl(v)
                v.tensor_tensor(a8, a8, c1s[:], ALU.add)
                v.tensor_tensor(b8, b8, c2s[:], ALU.add)
                vfl(v)
                v.tensor_single_scalar(a8, a8, CAP + 0.5, ALU.is_lt)
                v.tensor_single_scalar(b8, b8, CAP + 0.5, ALU.is_lt)
                vfl(v)
                v.tensor_tensor(a8, a8, p1c[:], ALU.mult)
                v.tensor_tensor(b8, b8, p2c[:], ALU.mult)
                vfl(v)
                s_wch.inc(v.tensor_tensor(w_sb[:], a8, b8, ALU.add))

        with ExitStack() as PS:
            ctmp = PS.enter_context(nc.sbuf_tensor("ctmp", [128, 2048], F32, side="right"))
            cslot = lambda u: ctmp[:, (u % 4) * 512:((u % 4) + 1) * 512]

            @block.tensor
            def _(t):
                t.wait_ge(s_qev.h, 32)
                t.wait_ge(s_x7d.h, 1)
                for m in range(ND):
                    for g2 in range(2):
                        u = m * 2 + g2
                        if u >= 4:
                            t.wait_ge(s_uev.h, u - 3)
                        bank = pb[u % 4]
                        jmax = 4 * g2 + 3
                        for j in range(jmax + 1):
                            rhs = (ones512[:] if j < 4 * g2
                                   else stairs[:, (j - 4 * g2) * 512:(j - 4 * g2 + 1) * 512])
                            ti = t.matmul(bank[:],
                                          bigA[:, PHIKV + j * 2048 + m * 128: PHIKV + j * 2048 + (m + 1) * 128],
                                          rhs, start=(j == 0), stop=(j == jmax))
                        s_umm.inc(ti)

            @block.vector
            def _(v):
                v.wait_ge(s_x7b.h, 1)
                v.wait_ge(s_qstt.h, 32)
                for m in range(ND):
                    for g2 in range(2):
                        u = m * 2 + g2
                        v.wait_ge(s_umm.h, u + 1)
                        bank = pb[u % 4]
                        v.scalar_tensor_tensor(
                            cslot(u), bank[:], ssel[:, m:m + 1],
                            bigA[:, PHIQ + m * 1024 + g2 * 512: PHIQ + m * 1024 + (g2 + 1) * 512],
                            ALU.add, ALU.mult)
                        vfl(v)
                        ti = v.tensor_scalar_mul(
                            bigB[:, m * 1024 + g2 * 512: m * 1024 + (g2 + 1) * 512],
                            cslot(u), SCALE)
                        s_uev.inc(ti)

        @block.tensor
        def _(t):
            t.wait_ge(s_g.h, 32)
            t.wait_ge(s_nev.h, 1)
            t.wait_ge(s_sev.h, 4)
            ti = t.matmul(pb[4][0:8, 0:8], stri64[0:8, 0:8], gn_sb[0:8, 0:8],
                          start=True, stop=True)
            t.matmul(pb[4][0:8, 8:16], aones8[0:8, 0:8], gn_sb[0:8, 0:8],
                     start=True, stop=False)
            ti = t.matmul(pb[4][0:8, 8:16], stri64[0:8, 0:8], gn_sb[0:8, 8:16],
                          start=False, stop=True)
            s_ob.inc(ti)
            for k in range(2):
                for e in range(E):
                    j = k * 8 + e
                    t.wait_ge(s_obe.h, j + 1)
                    ti = t.matmul(pb[5][0:1, 0:64], otab[0:8, k * 8 + e:k * 8 + e + 1],
                                  kron8[:], start=True, stop=True)
                    s_ob.inc(ti)
                    t.wait_ge(s_obe.h, j + 2)
                    ti = t.matmul(pb[4][0:128, 16:80], ones4f[0:1, :], orow[:],
                                  start=True, stop=True)
                    s_ob.inc(ti)

        @block.vector
        def _(v):
            v.wait_ge(s_tb.h, 64)
            v.wait_ge(s_ob.h, 1)
            vfl(v)
            v.tensor_copy(otab[0:8, 0:16], pb[4][0:8, 0:16])
            s_obe.inc(v.engine_nop())
            for k in range(2):
                src = g64a if k == 0 else g64b
                acc = g64c if k == 0 else t64d
                for e in range(E):
                    j = k * 8 + e
                    v.wait_ge(s_ob.h, 2 + 2 * j)
                    vfl(v)
                    v.tensor_copy(orow[:], pb[5][0:1, 0:64])
                    s_obe.inc(v.engine_nop())
                    v.wait_ge(s_ob.h, 3 + 2 * j)
                    v.tensor_single_scalar(eq64[:], src[:], float(e), ALU.is_equal)
                    vfl(v)
                    v.tensor_tensor(wrk64[:], eq64[:], pb[4][0:128, 16:80], ALU.mult)
                    vfl(v)
                    ti = v.tensor_tensor(acc[:], acc[:], wrk64[:], ALU.add)
                    vfl(v)
            v.tensor_tensor(g64c[:], g64c[:], t64d[:], ALU.min)
            vfl(v)
            s_kb.inc(v.tensor_single_scalar(kb64[:], g64c[:], CAP + 0.5, ALU.is_lt))
            v.wait_ge(s_wch.h, 1)
            vfl(v)
            s_kb.inc(v.tensor_single_scalar(oh12[:, 0:8], w_sb[:], 0.0, ALU.is_gt))

        with ExitStack() as PS:
            bo_r = PS.enter_context(nc.sbuf_tensor("bo_r", [1, D], BF16, side="right"))
            sstage = PS.enter_context(nc.sbuf_tensor("sstage", [128, 1024], BF16, side="right"))
            woS = Stream2(nc, CT, PS, "woS", [128, 512], BF16, nslots=4, pool=poolG)
            sslot = lambda u: sstage[:, (u % 2) * 512:((u % 2) + 1) * 512]



            @block.gpsimd
            def _(g):
                g.wait_ge(s_uev.h, 32)
                s_bo.inc(g.dma_start(out=bo_r[:], in_=brows_d[2:3, :]), 16)
                for eg in range(4):
                    for m in range(ND):
                        woS.load(g, wo_d[eg, m])

            @block.tensor
            def _(t):
                t.wait_ge(s_uev.h, 32)
                t.wait_ge(s_x7d.h, 1)
                t.wait_ge(s_nev.h, 1)
                t.wait_ge(s_cev.h, 8)
                t.wait_ge(s_kb.h, 1)
                t.wait_ge(s_bo.h, 16)
                for eg in range(4):
                    if eg > 0:
                        t.wait_ge(s_oev.h, eg * 8)
                    for m in range(ND):
                        buf = woS.acquire(t)
                        for tb in range(NT):
                            ti = t.matmul(pb[tb][:],
                                          bigB[:, m * 1024 + tb * 128: m * 1024 + (tb + 1) * 128],
                                          buf[:], start=(m == 0), stop=False)
                        woS.release(ti)
                    for tb in range(NT):
                        ti = t.matmul(pb[tb][:], ones512[0:1, 0:128],
                                      bo_r[0:1, eg * 512:(eg + 1) * 512],
                                      start=False, stop=True)
                        s_omm.inc(ti)

            @block.vector
            def _(v):
                for eg in range(4):
                    for tb in range(NT):
                        u = eg * 8 + tb
                        v.wait_ge(s_omm.h, u + 1)
                        if u >= 2:
                            v.wait_ge((s_std if u % 2 == 0 else s_std1).h, 16 * (u // 2))
                        ti = v.tensor_copy(sslot(u), pb[tb][:])
                        s_oev.inc(ti)

            @block.sync
            def _(q):
                for eg in range(4):
                    for tb in range(NT):
                        u = eg * 8 + tb
                        q.wait_ge(s_oev.h, u + 1)
                        (s_std if u % 2 == 0 else s_std1).inc(q.dma_start(
                            out=s_d[tb * 128:(tb + 1) * 128, eg * 512:(eg + 1) * 512],
                            in_=sslot(u)), 16)

        with ExitStack() as PS:
            cmp_sb = PS.enter_context(nc.sbuf_tensor("cmp_sb", [128, 2048], BF16, side="right"))
            slotrow = PS.enter_context(nc.sbuf_tensor("slotrow", [128, 512], F32, side="right"))

            @block.sync
            def _(q):
                q.wait_ge(s_std.h, 256)
                q.wait_ge(s_std1.h, 256)
                s_k.inc(q.dma_start(out=slotrow[:], in_=slotrow_d), 16)

            @block.tensor
            def _(t):
                t.wait_ge(s_oev.h, 32)
                t.wait_ge(s_kb.h, 2)
                kbo = oh12[:, 0:8]
                ti = t.matmul(pb[0][0:1, 0:64], onescol, kb64[:], start=True, stop=True)
                s_pgm.inc(ti)
                t.wait_ge(s_pge.h, 1)
                ti = t.transpose(pb[0][0:64, 64:65], wrk64[0:1, 0:64], ones4f[0:1, 0:1])
                s_pgm.inc(ti)
                t.wait_ge(s_pge.h, 2)
                ti = t.matmul(pb[0][0:64, 65:66], stri64[:], exr[0:64, 0:1], start=True, stop=True)
                s_pgm.inc(ti)
                t.wait_ge(s_pge.h, 3)
                ti = t.transpose(pb[0][0:1, 66:130], exr[0:64, 1:2], id64[:])
                s_pgm.inc(ti)
                ti = t.matmul(pb[0][0:1, 130:131], exr[0:64, 1:2], osel[:], start=True, stop=True)
                s_pgm.inc(ti)
                t.wait_ge(s_pge.h, 5)
                ti = t.matmul(pb[0][0:128, 131:132], ones4f[0:1, :], oor[0:1, 0:1], start=True, stop=True)
                s_pgm.inc(ti)
                t.matmul(pb[1][:, 0:64], triu01, kb64[:], start=True, stop=False)
                ti = t.matmul(pb[1][:, 0:64], ones4f[0:1, :], wrk64[0:1, 0:64], start=False, stop=True)
                s_pgm.inc(ti)
                ti = t.matmul(pb[2][0:1, 0:8], onescol, kbo, start=True, stop=True)
                s_pgm.inc(ti)
                t.wait_ge(s_pge.h, 8)
                ti = t.transpose(pb[2][0:8, 8:9], g64a[0:1, 0:8], ones4f[0:1, 0:1])
                s_pgm.inc(ti)
                t.wait_ge(s_pge.h, 9)
                ti = t.matmul(pb[2][0:8, 9:10], stri64[0:8, 0:8], g64b[0:8, 0:1], start=True, stop=True)
                s_pgm.inc(ti)
                t.wait_ge(s_pge.h, 10)
                ti = t.transpose(pb[2][0:1, 10:18], g64b[0:8, 1:2], id8[:])
                s_pgm.inc(ti)
                t.wait_ge(s_pge.h, 11)
                t.matmul(pb[2][:, 18:26], triu01, kbo, start=True, stop=False)
                ti = t.matmul(pb[2][:, 18:26], ones4f[0:1, :], g64a[0:1, 8:16], start=False, stop=True)
                s_pgm.inc(ti)
                t.wait_ge(s_pge.h, 13)
                for col in range(64):
                    t.wait_ge(s_cmpv.h, col + 1)
                    ti = t.matmul(pb[3][0:1, 0:512], onescol,
                                  cmp_sb[:, (col % 4) * 512:((col % 4) + 1) * 512],
                                  start=(col == 0), stop=(col == 63))
                    s_cmpm.inc(ti)

            @block.vector
            def _(v):
                v.wait_ge(s_pgm.h, 1)
                vfl(v)
                v.tensor_copy(wrk64[0:1, 0:64], pb[0][0:1, 0:64])
                s_pge.inc(v.engine_nop())
                v.wait_ge(s_pgm.h, 2)
                v.tensor_copy(exr[0:64, 0:1], pb[0][0:64, 64:65])
                s_pge.inc(v.engine_nop())
                v.wait_ge(s_pgm.h, 3)
                v.tensor_copy(exr[0:64, 1:2], pb[0][0:64, 65:66])
                s_pge.inc(v.engine_nop())
                v.wait_ge(s_pgm.h, 5)
                vfl(v)
                v.tensor_copy(wrk64[0:1, 0:64], pb[0][0:1, 66:130])
                s_pge.inc(v.engine_nop())
                v.tensor_copy(oor[0:1, 0:1], pb[0][0:1, 130:131])
                s_pge.inc(v.engine_nop())
                v.wait_ge(s_pgm.h, 6)
                v.tensor_copy(oocol[:], pb[0][0:128, 131:132])
                s_pge.inc(v.engine_nop())
                v.wait_ge(s_pgm.h, 8)
                vfl(v)
                v.tensor_copy(g64a[0:1, 0:8], pb[2][0:1, 0:8])
                s_pge.inc(v.engine_nop())
                s_pge.inc(v.engine_nop())
                v.wait_ge(s_pgm.h, 9)
                vfl(v)
                v.tensor_copy(g64b[0:8, 0:1], pb[2][0:8, 8:9])
                s_pge.inc(v.engine_nop())
                v.wait_ge(s_pgm.h, 10)
                vfl(v)
                v.tensor_copy(g64b[0:8, 1:2], pb[2][0:8, 9:10])
                s_pge.inc(v.engine_nop())
                v.wait_ge(s_pgm.h, 11)
                vfl(v)
                v.tensor_copy(g64a[0:1, 8:16], pb[2][0:1, 10:18])
                s_pge.inc(v.engine_nop())
                v.wait_ge(s_pgm.h, 7)
                vfl(v)
                v.tensor_copy(P64[:], pb[1][:, 0:64])
                s_pge.inc(v.engine_nop())
                s_pge.inc(v.engine_nop())
                v.wait_ge(s_pgm.h, 12)
                vfl(v)
                v.tensor_scalar(ro8[:], pb[2][:, 18:26], oocol[:], 1.0, ALU.add, ALU.subtract)
                vfl(v)
                v.tensor_scalar(ro8[:], ro8[:], 0.0, 4095.0, ALU.max, ALU.min)
                s_owv.inc(v.engine_nop())
                v.wait_ge(s_k.h, s_k.n)
                for col in range(64):
                    if col >= 4:
                        v.wait_ge(s_cmpm.h, col - 3)
                    vfl(v)
                    ti = v.tensor_tensor(
                        cmp_sb[:, (col % 4) * 512:((col % 4) + 1) * 512],
                        P64[:, col:col + 1].broadcast_to([128, 512]),
                        slotrow[:], ALU.is_le)
                    s_cmpv.inc(ti)
                v.wait_ge(s_cmpm.h, 64)
                vfl(v)
                ti = v.tensor_single_scalar(tokrow[:], pb[3][0:1, 0:512], 8191.0, ALU.min)
                s_toke.inc(ti)

            @block.sync
            def _(q):
                q.wait_ge(s_owv.h, 1)
                s_bnc.inc(q.dma_start(out=scr_r, in_=ro8[:]), 16)
                q.wait_ge(s_toke.h, 1)
                s_bnc.inc(q.dma_start(out=scr_t, in_=tokrow[:]), 16)
                q.wait_ge(s_bnc.h, 32)
                for h in range(2):
                    s_bnc.inc(q.dma_start(
                        out=idqf[16 * h:16 * (h + 1), :],
                        in_=scr_t[0:1, 0:512].rearrange("o (c p) -> (o p) c", p=16),
                        allow_slow_non_contiguous=True), 16)
                    s_bnc.inc(q.dma_start(
                        out=idof[16 * h:16 * (h + 1), :].rearrange("q (a b) -> q a b", b=8),
                        in_=scr_r.rearrange("(b q) a -> q a b", q=16),
                        allow_slow_non_contiguous=True), 16)

            @block.vector
            def _(v):
                v.wait_ge(s_bnc.h, 96)
                vfl(v)
                v.memset(idq[:], 0)
                v.memset(ido[:], 0)
                vfl(v)
                v.tensor_copy(idq[0:32, :], idqf[:])
                ti = v.tensor_copy(ido[0:32, :], idof[:])
                s_idx.inc(ti)

        with ExitStack() as PS:
            _credit = [(p.h, p.n) for p in poolG.ld + poolT.ld + poolS.ld + poolU.ld]
            x2S = Stream2(nc, CT, PS, "x2S", [128, D // 2], F32, nslots=2, pool=poolS)
            s2S = Stream2(nc, CT, PS, "s2S", [128, D // 2], BF16, nslots=2, pool=poolU)
            b2_r = PS.enter_context(nc.sbuf_tensor("b2_r", [1, D], BF16, side="right"))
            fpub = PS.enter_context(nc.sbuf_tensor("fpub", [128, 1024], BF16, side="right"))
            wft = PS.enter_context(nc.sbuf_tensor("wft", [128, 1024], F32, side="right"))
            w1S = Stream2(nc, CT, PS, "w1S", [128, 2048], BF16, nslots=2, pool=poolG)
            w2S = Stream2(nc, CT, PS, "w2S", [128, 1024], BF16, nslots=2, pool=poolT)
            sg = PS.enter_context(nc.sbuf_tensor("sg", [128, D // 2], F32, side="right"))
            ot = PS.enter_context(nc.sbuf_tensor("ot", [128, D], F32, side="right"))
            oslot = lambda u: ot[:, (u % 2) * 1024:((u % 2) + 1) * 1024]

            @block.sync
            def _(q):
                q.wait_ge(s_std.h, 256)
                q.wait_ge(s_std1.h, 256)
                q.wait_ge(s_omm.h, 32)
                for tb in range(2):
                    x2S.load(q, x_d[tb * 128:(tb + 1) * 128, 0:D // 2])
                    s2S.load(q, s_d[tb * 128:(tb + 1) * 128, 0:D // 2])
            fslot = lambda u: fpub[:, (u % 2) * 512:((u % 2) + 1) * 512]
            wslot = lambda u: wft[:, (u % 2) * 512:((u % 2) + 1) * 512]

            @block.gpsimd
            def _(g):
                g.wait_ge(s_oev.h, 32)
                g.wait_ge(s_std.h, 256)
                g.wait_ge(s_std1.h, 256)
                g.wait_ge(s_cmpm.h, 64)
                s_b2.inc(g.dma_start(out=b2_r[:], in_=brows_d[3:4, :]), 16)
                g.wait_ge(s_ccx.h, 1)
                g.wait_ge(s_idx.h, 1)
                g.wait_ge(s_omm.h, 32)
                s_xg.inc(g.dma_gather(
                    bigB[:, 0:8192].rearrange("p (f i) -> p f i", i=512),
                    ccx_out.ap()[:, :], idq[:], 512, 512, D, transpose=True), 16)
                for fc in range(NF):
                    w1S.load(g, w1_d[fc])
                for u in range(2 * NF):
                    eh, fc = divmod(u, NF)
                    if u == NF + 10:
                        g.wait_ge(s_fpd.h, 64)
                        g.wait_ge(s_fpd1.h, 64)
                        s_cc3.inc(g.collective_compute(
                            "AllGather", ALU.bypass,
                            replica_groups=[list(range(NCORE))],
                            ins=[cc3_in[0].ap().opt()],
                            outs=[cc3_out[0].ap().opt()]))
                    if u == NF + 34:
                        g.wait_ge(s_cc3.h, 1)
                        g.wait_ge(s_amm.h, NF)
                        s_fg.inc(g.dma_gather(
                            bigB[:, 8192:16384].rearrange("p (b e) -> p b e", e=D // 2),
                            cc3_out[0].ap()[:, :], ido[:], 1024, 1024, D // 2,
                            transpose=False), 16)
                    w2S.load(g, w2_d[fc, :, eh * 1024:(eh + 1) * 1024])
                g.wait_ge(s_fpd.h, 128)
                g.wait_ge(s_fpd1.h, 128)
                s_cc3b.inc(g.collective_compute(
                    "AllGather", ALU.bypass,
                    replica_groups=[list(range(NCORE))],
                    ins=[cc3_in[1].ap().opt()],
                    outs=[cc3_out[1].ap().opt()]))
                g.wait_ge(s_cc3b.h, 1)
                g.wait_ge(s_bmm.h, 16)
                s_fgb.inc(g.dma_gather(
                    bigA[:, 0:8192].rearrange("p (b e) -> p b e", e=D // 2),
                    cc3_out[1].ap()[:, :], ido[:], 1024, 1024, D // 2,
                    transpose=False), 16)

            @block.tensor
            def _(t):
                t.wait_ge(s_xg.h, 16)
                for fc in range(NF):
                    buf = w1S.acquire(t)
                    if fc >= 4:
                        t.wait_ge(s_aev.h, fc - 3)
                    bank = pb[fc % 4]
                    for m in range(ND):
                        ti = t.matmul(bank[:], buf[:, m * 128:(m + 1) * 128],
                                      bigB[:, m * 512:(m + 1) * 512],
                                      start=(m == 0), stop=(m == ND - 1))
                    s_amm.inc(ti)

            @block.scalar
            def _(sc):
                sc.wait_ge(s_uev.h, 32)
                for fc in range(NF):
                    sc.wait_ge(s_amm.h, fc + 1)
                    ti = sc.activation(bigA[:, fc * 512:(fc + 1) * 512],
                                       pb[fc % 4][:], AF.Gelu,
                                       bias=b1c[:, fc:fc + 1], scale=1.0)
                    w1S.release(ti)
                    s_aev.inc(sc.copy(flc4[fc % 4], epsc))

            @block.tensor
            def _(t):
                t.wait_ge(s_aev.h, NF)
                t.wait_ge(s_b2.h, 16)
                for eh in range(2):
                    if eh > 0:
                        t.wait_ge(s_fpv.h, 8)
                    for fc in range(NF):
                        buf = w2S.acquire(t)
                        for tb4 in range(4):
                            for eg2 in range(2):
                                ti = t.matmul(pb[tb4 * 2 + eg2][:],
                                              bigA[:, fc * 512 + tb4 * 128: fc * 512 + (tb4 + 1) * 128],
                                              buf[:, eg2 * 512:(eg2 + 1) * 512],
                                              start=(fc == 0), stop=False)
                        w2S.release(ti)
                    for tb4 in range(4):
                        for eg2 in range(2):
                            ti = t.matmul(pb[tb4 * 2 + eg2][:], ones512[0:1, 0:128],
                                          b2_r[0:1, eh * 1024 + eg2 * 512: eh * 1024 + (eg2 + 1) * 512],
                                          start=False, stop=True)
                            s_bmm.inc(ti)

            @block.vector
            def _(v):
                for tb4 in range(4):
                    for eg2 in range(2):
                        uu = tb4 * 2 + eg2
                        v.wait_ge(s_bmm.h, uu + 1)
                        if uu >= 2:
                            v.wait_ge((s_fpd if uu % 2 == 0 else s_fpd1).h,
                                      16 * (uu // 2))
                        vfl(v)
                        ti = v.tensor_copy(fslot(uu), pb[uu][:])
                        s_fpv.inc(ti)
                v.wait_ge(s_fg.h, 16)
                v.wait_ge(s_wch.h, 1)
                for tb in range(NT):
                    for eg2 in range(2):
                        u = tb * 2 + eg2
                        if u >= 2:
                            v.wait_ge(s_bev.h, u - 1)
                        vfl(v)
                        ti = v.tensor_scalar_mul(
                            wslot(u),
                            bigB[:, 8192 + tb * 1024 + eg2 * 512:
                                 8192 + tb * 1024 + (eg2 + 1) * 512],
                            w_sb[:, tb:tb + 1])
                        s_gw.inc(ti)
                for tb4 in range(4):
                    for eg2 in range(2):
                        u8 = tb4 * 2 + eg2
                        uu = 8 + u8
                        v.wait_ge(s_bmm.h, uu + 1)
                        v.wait_ge((s_fpd if uu % 2 == 0 else s_fpd1).h,
                                  16 * (uu // 2))
                        vfl(v)
                        ti = v.tensor_copy(fslot(uu), pb[u8][:])
                        s_fpv.inc(ti)

            @block.sync
            def _(q):
                for eh in range(2):
                    for tb4 in range(4):
                        for eg2 in range(2):
                            uu = eh * 8 + tb4 * 2 + eg2
                            q.wait_ge(s_fpv.h, uu + 1)
                            (s_fpd if uu % 2 == 0 else s_fpd1).inc(q.dma_start(
                                out=cc3_in[eh].ap()[tb4 * 128:(tb4 + 1) * 128,
                                                    eg2 * 512:(eg2 + 1) * 512],
                                in_=fslot(uu)), 16)

            @block.scalar
            def _(sc):
                sc.wait_ge(s_omm.h, 32)
                for h in range(2):
                    for tb in range(NT):
                        for eg2 in range(2):
                            u = h * 16 + tb * 2 + eg2
                            sc.wait_ge(s_gw.h, u + 1)
                            ti = sc.activation(
                                bigB[:, tb * 2048 + h * 1024 + eg2 * 512:
                                     tb * 2048 + h * 1024 + (eg2 + 1) * 512],
                                wslot(u), AF.Silu)
                            s_bev.inc(ti)

            def _uslc(u):
                h, tb = divmod(u, NT)
                return (tb * 128, (tb + 1) * 128, h * (D // 2), (h + 1) * (D // 2))

            @block.sync
            def _(q):
                for u in range(2, 16):
                    r0, r1, c0, c1 = _uslc(u)
                    x2S.load(q, x_d[r0:r1, c0:c1])
                    s2S.load(q, s_d[r0:r1, c0:c1])
                    q.wait_ge(s_fin.h, u - 1)
                    r0, r1, c0, c1 = _uslc(u - 2)
                    (s_out if u % 2 == 0 else s_out1).inc(
                        q.dma_start(out=out_d[r0:r1, c0:c1], in_=oslot(u - 2)), 16)
                for u in range(15, 17):
                    q.wait_ge(s_fin.h, u)
                    r0, r1, c0, c1 = _uslc(u - 1)
                    (s_out if u % 2 == 1 else s_out1).inc(
                        q.dma_start(out=out_d[r0:r1, c0:c1], in_=oslot(u - 1)), 16)
                q.wait_ge(s_out.h, s_out.n)
                q.wait_ge(s_out1.h, s_out1.n)

            @block.vector
            def _(v):
                for _h, _n in _credit:
                    if _n:
                        v.wait_ge(_h, _n)
                vfl(v)
                for u in range(16):
                    h, tb = divmod(u, NT)
                    if h == 1:
                        if tb == 0:
                            v.wait_ge(s_fgb.h, 16)
                        for eg2 in range(2):
                            k = 16 + tb * 2 + eg2
                            v.wait_ge(s_bev.h, k - 1)
                            vfl(v)
                            ti = v.tensor_scalar_mul(
                                wslot(k),
                                bigA[:, tb * 1024 + eg2 * 512:
                                     tb * 1024 + (eg2 + 1) * 512],
                                w_sb[:, tb:tb + 1])
                            s_gw.inc(ti)
                    v.wait_ge(s_bev.h, h * 16 + tb * 2 + 2)
                    v.wait_ge(x2S.ld[u % 2].h, x2S.ldb[u % 2] + 16 * (u // 2 + 1))
                    v.wait_ge(s2S.ld[u % 2].h, s2S.ldb[u % 2] + 16 * (u // 2 + 1))
                    vfl(v)
                    buf = x2S.bufs[u % 2]
                    sbuf_t = s2S.bufs[u % 2]
                    ti = v.tensor_tensor(
                        sg[:], sbuf_t[:],
                        bigB[:, tb * 2048 + h * 1024: tb * 2048 + (h + 1) * 1024],
                        ALU.mult)
                    s2S.release(ti)
                    if u >= 2:
                        v.wait_ge((s_out if u % 2 == 0 else s_out1).h,
                                  16 * (u // 2))
                    vfl(v)
                    ti = v.tensor_tensor(oslot(u), buf[:], sg[:], ALU.add)
                    s_fin.inc(ti)
                    x2S.release(v.engine_nop())

    return nc


def _stage_host(inputs):
    f = np.float32
    asf = lambda a: np.asarray(a, dtype=f)
    x = asf(inputs["L0_attn_x"]).reshape(NCORE, TPC, D)
    WqT = np.ascontiguousarray(asf(inputs["L0_attn_Wq"]).T)
    WkT = np.ascontiguousarray(asf(inputs["L0_attn_Wk"]).T)
    WvT = np.ascontiguousarray(asf(inputs["L0_attn_Wv"]).T)
    WoT = np.ascontiguousarray(asf(inputs["L0_attn_Wo"]).T)
    w1T = np.ascontiguousarray(asf(inputs["L0_ffn_w1"]).T)
    w2T = np.ascontiguousarray(asf(inputs["L0_ffn_w2"]).T)
    rT = np.ascontiguousarray(asf(inputs["L0_ffn_router_w"]).T)

    wq = np.ascontiguousarray(WqT.reshape(ND, 128, ND, 128).transpose(2, 1, 0, 3))
    wg = lambda W: np.ascontiguousarray(W.reshape(ND, 128, 4, 512).transpose(2, 0, 1, 3))
    wk, wv, wo = wg(WkT), wg(WvT), wg(WoT)
    w1 = np.ascontiguousarray(w1T.reshape(ND, 128, NF, 128).transpose(2, 1, 0, 3))
    w2 = np.ascontiguousarray(w2T.reshape(NF, 128, D))
    rt = np.ascontiguousarray(rT.reshape(ND, 128, E).transpose(1, 0, 2).reshape(128, ND * E))
    bq = np.ascontiguousarray(asf(inputs["L0_attn_bq"]).reshape(ND, 128).T)
    b1 = np.ascontiguousarray(asf(inputs["L0_ffn_b1"]).reshape(NF, 128).T)
    brows = np.ascontiguousarray(np.stack(
        [asf(inputs["L0_attn_bk"]), asf(inputs["L0_attn_bv"]),
         asf(inputs["L0_attn_bo"]), asf(inputs["L0_ffn_b2"])]))

    stairs = np.zeros((128, 4 * 512), dtype=bfloat16)
    tt = np.arange(128)[:, None]
    cc = np.arange(512)[None, :]
    for k in range(4):
        stairs[:, k * 512:(k + 1) * 512] = ((128 * k + tt) <= cc).astype(bfloat16)
    ones512 = np.ones((128, 512), dtype=bfloat16)
    ones4f = np.ones((4, 128), dtype=f)
    id8 = np.eye(8, dtype=f)
    iota = np.zeros((128, 128), dtype=f)
    iota[:, 0:64] = np.tile(np.arange(8, dtype=f), 8)[None, :]
    iota[:, 64:128] = np.tile(np.arange(8, dtype=f) - BIG, 8)[None, :]

    stri64 = np.triu(np.ones((64, 64), dtype=f), k=1)
    id64 = np.eye(64, dtype=f)
    kron8 = np.zeros((8, 64), dtype=f)
    for c in range(8):
        kron8[c, c * 8:(c + 1) * 8] = 1.0
    shared = {"wq": wq, "wk": wk, "wv": wv, "wo": wo, "w1": w1, "w2": w2,
              "rt_i": rt, "bq_i": bq, "b1_i": b1, "brows": brows,
              "stairs_i": stairs, "ones512_i": ones512, "ones4f_i": ones4f,
              "id8_i": id8, "iota_i": iota, "stri64_i": stri64, "id64_i": id64,
              "kron8_i": kron8}
    in_maps = []
    for c in range(NCORE):
        selv = np.zeros((8, 1), f)
        if c % 2 == 1:
            selv[c - 1, 0] = 1.0
        presel = np.zeros((8, 128), f)
        presel[:c, :] = 1.0
        osel = np.zeros((64, 1), f)
        osel[c * 8, 0] = 1.0
        slotrow = np.tile(c * 512.0 + np.arange(512, dtype=f), (128, 1))
        m = dict(shared)
        m.update({"x": np.ascontiguousarray(x[c]), "selc_i": selv,
                  "presel_i": presel, "aones8_i": np.ones((8, 128), f),
                  "slotrow_i": slotrow, "osel_i": osel})
        in_maps.append(m)
    return in_maps


def kernel(**inputs):
    import bass_rust as _br
    import concourse.library_config as _lc
    in_maps = _stage_host(inputs)
    nc = build_nc()
    _mask = {}
    for _lib in _lc.all_libraries:
        for _it in _lib.instructions:
            _mask[_it] = _mask.get(_it, 0) | (1 << _lib.index)
    _br.insert_library_loads(nc, _mask, len(_lc.all_libraries), _lc.standard.index)
    mybir.codegen_inst_isa_subclasses(nc)
    res = _RUN(nc, in_maps, core_ids=list(range(NCORE)))
    out = np.concatenate([r["out"] for r in res.results], axis=0)
    return out.reshape(4, 2048, 2048).astype(np.float32)
